# version 06
# baseline (speedup 1.0000x reference)
import numpy as np
import ml_dtypes
import concourse.bass as bass
import concourse.mybir as mybir
from concourse.bass_utils import run_bass_kernel_spmd

F32 = mybir.dt.float32
F32R = mybir.dt.float32r
BF16 = mybir.dt.bfloat16
AF = mybir.ActivationFunctionType
OP = mybir.AluOpType

S = 4096
D = 1024
CH = 128
TOWN = 2048
ABLK = 512
BBLK = 256
NBB = S // BBLK
NE = 8
EPS = 1e-5

ENGINES = ("tensor", "vector", "scalar", "sync", "gpsimd")
SEMS = ("SW", "ST", "SS", "SV", "SO", "SX0", "SX1", "SX2", "SX3", "SG0", "SG1", "SU0", "SU1", "SD0", "SD1", "SO0", "SO1", "SO2", "SO3", "SO4", "SO5", "SO6", "SO7")


class Builder:
    def __init__(self):
        self.prog = {e: [] for e in ENGINES}
        self.cnt = {s: 0 for s in SEMS}
        self.lastwait = {}
        self.sems = None
        self.bank_free = {b: None for b in range(10)}
        self.bank_done = {b: None for b in range(10)}

    def emit(self, eng, fn):
        self.prog[eng].append(fn)

    def wait(self, eng, sem, val):
        if val is None or val <= 0:
            return
        key = (eng, sem)
        if self.lastwait.get(key, 0) >= val:
            return
        self.lastwait[key] = val
        self.emit(eng, lambda E, s=self.sems[sem], v=val: E.wait_ge(s, v))

    def wait_pair(self, eng, pair):
        if pair is not None:
            self.wait(eng, pair[0], pair[1])

    def inc(self, sem, n=1):
        self.cnt[sem] += n
        return self.cnt[sem]


def build_program():
    nc = bass.Bass("TRN2", target_bir_lowering=False)
    B = Builder()

    def din(name, shape, dtype=F32R):
        return nc.dram_tensor(name, list(shape), dtype, kind="ExternalInput").ap()

    x0tD = din("x0t_d", (D, S))
    xbtD = din("xbt_d", (D, S))
    xotD = din("xot_d", (D, TOWN))
    wingD = din("wing_d", (24, 128, 1024))
    windtD = din("windt_d", (128, 128))
    woutD = din("wout_d", (8, 128, 1024))
    routerD = din("router_d", (128, 64))
    sgD = din("sg_d", (16, 128, 1024))
    suD = din("su_d", (16, 128, 1024))
    sdaD = din("sda_d", (16, 128, 768))
    sdbD = din("sdb_d", (16, 128, 256))
    wgD = din("wg_d", (NE, 16, 128, 1024))
    wuD = din("wu_d", (NE, 16, 128, 1024))
    wdaD = din("wda_d", (NE, 16, 128, 768))
    wdbD = din("wdb_d", (NE, 16, 128, 256))
    convwD = din("convw_d", (128, 96), F32)
    convbD = din("convb_d", (128, 24), F32)
    aparamD = din("aparam_d", (16, 1), F32)
    dtbiasD = din("dtbias_d", (16, 1), F32)
    rbiasD = din("rbias_d", (8, 1), F32)
    id128bD = din("id128b_d", (128, 128), BF16)
    id128fD = din("id128f_d", (128, 128), F32)
    id16fD = din("id16f_d", (16, 16), F32)
    id8fD = din("id8f_d", (8, 8), F32)
    trimaskD = din("trimask_d", (128, 128), F32)
    onesrowD = din("onesrow_d", (1, 128), F32R)
    onescolD = din("onescol_d", (128, 1), F32R)
    lselD = din("lsel_d", (16, 128), F32)
    selmaskD = din("selmask_d", (16, 8), F32)
    sconstD = din("sconst_d", (128, 6), F32)
    sel8D = din("sel8_d", (8, 1024), F32R)
    outcD = nc.dram_tensor("outc", [D, S], F32, kind="ExternalOutput").ap()

    al = nc.alloc_sbuf_tensor
    at = nc.alloc_sbuf_tensor_at
    baseA, _ = nc.bump_sbuf(8 * TOWN * 4)
    baseH, _ = nc.bump_sbuf(16 * ABLK * 4)
    xn_own = at("xn_own", [128, 8 * TOWN], F32R, offset=baseA)
    h_sb = at("h_sb", [128, 16 * ABLK], F32R, offset=baseH)
    o = baseA
    xbcn = at("xbcn", [128, 24 * (BBLK + 3)], BF16, offset=o); o += 24 * (BBLK + 3) * 2
    o = (o + 31) // 32 * 32
    convo = at("convo", [128, 24 * BBLK], BF16, offset=o); o += 24 * BBLK * 2
    o = (o + 31) // 32 * 32
    xt_sb = at("xt_sb", [128, 8 * 128], BF16, offset=o); o += 8 * 128 * 2
    bdt_sb = at("bdt_sb", [128, 8 * 128], BF16, offset=o); o += 8 * 128 * 2
    gmt_sb = at("gmt_sb", [128, 16 * 128], BF16, offset=o); o += 16 * 128 * 2
    ztm = at("ztm", [128, 1024], BF16, offset=o); o += 1024 * 2
    rb16 = at("rb16", [128, 8 * 64], BF16, offset=o); o += 8 * 64 * 2
    o = (o + 31) // 32 * 32
    negmb = at("negmb", [1, BBLK], F32R, offset=o); o += BBLK * 4
    msqb = at("msqb", [1, BBLK], F32, offset=o); o += BBLK * 4
    varb = at("varb", [1, BBLK], F32, offset=o); o += BBLK * 4
    sdb = at("sdb", [1, BBLK], F32, offset=o); o += BBLK * 4
    rbrow = at("rbrow", [1, BBLK], F32R, offset=o); o += BBLK * 4
    mb2row = at("mb2row", [1, BBLK], F32, offset=o); o += BBLK * 4
    rbbc = at("rbbc", [128, BBLK], F32, offset=o); o += BBLK * 4
    dtr0_sb = at("dtr0_sb", [16, BBLK], F32, offset=o); o += BBLK * 4
    dt0_sb = at("dt0_sb", [16, BBLK], F32, offset=o); o += BBLK * 4
    a0_sb = at("a0_sb", [16, BBLK], F32, offset=o); o += BBLK * 4
    dtrb_sb = at("dtrb_sb", [16, BBLK], F32, offset=o); o += BBLK * 4
    dtb_sb = at("dtb_sb", [16, BBLK], F32, offset=o); o += BBLK * 4
    eab_sb = at("eab_sb", [16, BBLK], F32, offset=o); o += BBLK * 4
    norm_sb = at("norm_sb", [16, BBLK], F32, offset=o); o += BBLK * 4
    lnn_sb = at("lnn_sb", [16, BBLK], F32, offset=o); o += BBLK * 4
    cuma = at("cuma", [16, CH], F32, offset=o); o += CH * 4
    fexp = at("fexp", [16, CH], F32, offset=o); o += CH * 4
    dexp = at("dexp", [16, CH], F32, offset=o); o += CH * 4
    psiin = at("psiin", [16, CH], F32, offset=o); o += CH * 4
    psie = at("psie", [16, CH], F32, offset=o); o += CH * 4
    lam = at("lam", [16, 1], F32, offset=o); o += 32
    lamsel = at("lamsel", [16, 8], F32, offset=o); o += 32
    assert o - baseA <= 8 * TOWN * 4
    o = baseH
    xn0 = at("xn0", [128, 8 * BBLK], F32R, offset=o); o += 8 * BBLK * 4
    zfm = at("zfm", [128, 8 * BBLK], F32R, offset=o); o += 8 * BBLK * 4
    outc_sb = at("outc_sb", [128, 8 * BBLK], F32, offset=o); o += 8 * BBLK * 4
    r_sb = at("r_sb", [128, 8 * 64], F32, offset=o); o += 8 * 64 * 4
    ft_sb = at("ft_sb", [128, 32], F32, offset=o); o += 32 * 4
    dtt_sb = at("dtt_sb", [128, 32], F32, offset=o); o += 32 * 4
    psit_sb = at("psit_sb", [128, 32], F32, offset=o); o += 32 * 4
    lamx_sb = at("lamx_sb", [128, 16], F32, offset=o); o += 16 * 4
    assert o - baseH <= 16 * ABLK * 4

    id128b = al("id128b", [128, 128], BF16)
    id128f = al("id128f", [128, 128], F32)
    id16f = al("id16f", [16, 16], F32)
    id8f = al("id8f", [8, 8], F32)
    trimask = al("trimask", [128, 128], F32)
    onesrow = al("onesrow", [1, 128], F32R)
    onescol = al("onescol", [128, 1], F32R)
    lsel = al("lsel", [16, 128], F32)
    selmask = al("selmask", [16, 8], F32)
    sel8 = al("sel8", [8, 1024], F32R)
    sconst = al("sconst", [128, 6], F32)
    convw = al("convw", [128, 96], F32)
    convb = al("convb", [128, 24], F32)
    aparam = al("aparam", [16, 1], F32)
    dtbias = al("dtbias", [16, 1], F32)
    rbias = al("rbias", [8, 1], F32)
    ones16 = al("ones16", [16, 128], F32)
    windt = al("windt", [128, 128], F32R)
    router = al("router", [128, 64], F32R)
    w1dt = al("w1dt", [1, 16], F32R)
    acc = al("acc", [128, 8 * TOWN], BF16)
    xk = al("xk", [128, 2 * ABLK], F32R)
    sq = al("sq", [128, 2 * ABLK], F32R)
    rgw = al("rgw", [128, 2 * 1024], F32R)
    rgu = al("rgu", [128, 2 * 1024], F32R)
    rgd = al("rgd", [128, 2 * 1024], F32R)
    silu_sb = al("silu_sb", [128, 2 * ABLK], F32)
    tmpA = al("tmpA", [128, ABLK], F32)
    wbc = al("wbc", [128, ABLK], F32)
    probs_fm = al("probs_fm", [8, ABLK], F32)
    ptm = al("ptm", [128, 8], F32)
    p1col = al("p1col", [128, 1], F32)
    p2col = al("p2col", [128, 1], F32)
    s12 = al("s12", [128, 1], F32)
    rs12 = al("rs12", [128, 1], F32)
    eqm = al("eqm", [128, 8], F32)
    wtm = al("wtm", [128, 8], F32)
    wT = al("wT", [8, TOWN], F32R)
    mrow = al("mrow", [1, ABLK], F32R)
    msqrow = al("msqrow", [1, ABLK], F32)
    varrow = al("varrow", [1, ABLK], F32)
    sdrow = al("sdrow", [1, ABLK], F32)
    rrow = al("rrow", [1, ABLK], F32R)
    m2row = al("m2row", [1, ABLK], F32)
    mbc = al("mbc", [128, ABLK], F32)
    rbc = al("rbc", [128, ABLK], F32)
    carry = al("carry", [16, 1], F32)

    pb = [nc.alloc_psum_tensor(f"pb{i}", [128, 512], F32) for i in range(6)]
    pbB = [nc.alloc_psum_tensor(f"pbB{i}", [128, 1024], BF16) for i in range(2)]
    sems = {s: nc.alloc_semaphore(s) for s in SEMS}
    B.sems = sems

    def dma(eng, sem, out_ap, in_ap):
        v = B.inc(sem, 16)
        B.emit(eng, lambda E, oa=out_ap, ia=in_ap, sh=sems[sem]:
               E.dma_start(out=oa, in_=ia).then_inc(sh, 16))
        return v

    def mm(out_ap, lhsT, rhs, start, stop, inc=False):
        v = B.inc("ST") if inc else None
        def fn(E, oa=out_ap, l=lhsT, r=rhs, st=start, sp=stop, sh=sems["ST"], vv=v):
            inst = E.matmul(oa, l, r, start=st, stop=sp)
            if vv is not None:
                inst.then_inc(sh, 1)
        B.emit("tensor", fn)
        return v

    def tp(out_ap, in_ap, ident, inc=False):
        v = B.inc("ST") if inc else None
        def fn(E, oa=out_ap, ia=in_ap, idn=ident, sh=sems["ST"], vv=v):
            inst = E.transpose(oa, ia, idn)
            if vv is not None:
                inst.then_inc(sh, 1)
        B.emit("tensor", fn)
        return v

    lastw = {}
    lastr = {}
    WKEYS = ("out", "ap", "accum_out")

    def _tname(v):
        t = getattr(v, "tensor", None)
        return getattr(t, "name", None) if t is not None else None

    def _autosync(eng, kw):
        rnames, wnames = [], []
        for k, v in kw.items():
            if isinstance(v, (int, float, str, bool)) or v is None:
                continue
            n = _tname(v)
            if n is None:
                continue
            (wnames if k in WKEYS else rnames).append(n)
        for n in rnames:
            if n in lastw:
                B.wait_pair(eng, lastw[n])
        for n in wnames:
            if n in lastw:
                B.wait_pair(eng, lastw[n])
            if n in lastr:
                B.wait_pair(eng, lastr[n])

    def _markw(eng, sem, kw, v):
        for k, val in kw.items():
            if isinstance(val, (int, float, str, bool)) or val is None:
                continue
            n = _tname(val)
            if n is None:
                continue
            if k in WKEYS:
                lastw[n] = (sem, v)
            else:
                lastr[n] = (sem, v)

    def vec(name, inc=False, **kw):
        _autosync("vector", kw)
        v = B.inc("SV")
        def fn(E, n=name, kw=kw, sh=sems["SV"], vv=v):
            getattr(E, n)(**kw).then_inc(sh, 1)
        B.emit("vector", fn)
        _markw("vector", "SV", kw, v)
        return v

    def vscan(out_ap, d0, d1, ini, inc=True):
        kw = {"out": out_ap, "d0": d0, "d1": d1, "ini": ini}
        _autosync("vector", kw)
        v = B.inc("SV")
        def fn(E, oa=out_ap, a=d0, b=d1, i=ini, sh=sems["SV"], vv=v):
            E.tensor_tensor_scan(oa, a, b, i, OP.mult, OP.add).then_inc(sh, 1)
        B.emit("vector", fn)
        _markw("vector", "SV", kw, v)
        return v

    def sca(name, inc=False, **kw):
        _autosync("scalar", kw)
        v = B.inc("SS")
        def fn(E, n=name, kw=kw, sh=sems["SS"], vv=v):
            getattr(E, n)(**kw).then_inc(sh, 1)
        B.emit("scalar", fn)
        _markw("scalar", "SS", kw, v)
        return v

    def tw(sem, val):
        B.wait("tensor", sem, val)

    def bank_w(b):
        B.wait_pair("tensor", B.bank_free[b])

    def consume(eng, b):
        B.wait_pair(eng, B.bank_done[b])

    X = mybir.AxisListType.X

    for sb_, dr in [(id128b, id128bD), (id128f, id128fD), (id16f, id16fD),
                    (id8f, id8fD), (trimask, trimaskD), (onesrow, onesrowD),
                    (onescol, onescolD), (lsel, lselD), (selmask, selmaskD), (sel8, sel8D), (sconst, sconstD),
                    (convw, convwD), (convb, convbD), (aparam, aparamD),
                    (dtbias, dtbiasD), (rbias, rbiasD), (windt, windtD),
                    (router, routerD)]:
        wconst = dma("sync", "SW", sb_[:], dr)
    tw("SW", wconst)
    B.wait("vector", "SW", wconst)
    B.wait("scalar", "SW", wconst)
    vec("memset", ap=ones16[:], constant=1.0)
    vec("memset", inc=True, ap=carry[:], constant=0.0)

    bank_w(0)
    for k in range(8):
        v = mm(pb[0][0:1, 0:16], onescol[:], windt[:, k * 16:(k + 1) * 16],
               start=(k == 0), stop=(k == 7), inc=(k == 7))
    B.bank_done[0] = ("ST", v)
    consume("scalar", 0)
    vs = sca("activation", inc=True, out=w1dt[:], in_=pb[0][0:1, 0:16], func=AF.Copy)
    B.bank_free[0] = ("SS", vs)

    slot_free = {i: None for i in range(4)}
    for blk in range(4):
        c0 = blk * ABLK
        for k in range(8):
            sl = k % 2
            B.wait_pair("gpsimd", slot_free[sl])
            vx = dma("gpsimd", f"SX{sl}", xk[:, sl * ABLK:(sl + 1) * ABLK],
                     xotD[k * 128:(k + 1) * 128, c0:c0 + ABLK])
            B.wait("scalar", f"SX{sl}", vx)
            vq = sca("activation", inc=True, out=sq[:, sl * ABLK:(sl + 1) * ABLK],
                     in_=xk[:, sl * ABLK:(sl + 1) * ABLK], func=AF.Square)
            tw(f"SX{sl}", vx)
            if k == 0:
                bank_w(0); bank_w(1)
            mm(pb[0][0:1, 0:ABLK], onescol[:], xk[:, sl * ABLK:(sl + 1) * ABLK],
               start=(k == 0), stop=(k == 7))
            B.wait("tensor", "SS", vq)
            v1 = mm(pb[1][0:1, 0:ABLK], onescol[:], sq[:, sl * ABLK:(sl + 1) * ABLK],
                    start=(k == 0), stop=(k == 7), inc=True)
            slot_free[sl] = ("ST", v1)
        B.bank_done[0] = ("ST", v1); B.bank_done[1] = ("ST", v1)
        consume("scalar", 0); consume("scalar", 1)
        sca("activation", out=mrow[:], in_=pb[0][0:1, 0:ABLK], func=AF.Copy, scale=sconst[0:1, 1:2])
        vs = sca("activation", inc=True, out=msqrow[:], in_=pb[1][0:1, 0:ABLK],
                 func=AF.Copy, scale=sconst[0:1, 1:2])
        B.bank_free[0] = ("SS", vs); B.bank_free[1] = ("SS", vs)
        B.wait("scalar", "SS", vs)
        vs2 = sca("activation", inc=True, out=m2row[:], in_=mrow[:], func=AF.Square)
        B.wait("vector", "SS", vs2)
        vv = vec("tensor_sub", inc=True, out=varrow[:], in0=msqrow[:], in1=m2row[:])
        B.wait("scalar", "SV", vv)
        vln1 = sca("activation", inc=True, out=sdrow[:], in_=varrow[:], func=AF.Ln,
                   bias=sconst[0:1, 0:1])
        B.wait("scalar", "SS", vln1)
        vs3 = sca("activation", inc=True, out=rrow[:], in_=sdrow[:], func=AF.Exp,
                  scale=sconst[0:1, 5:6])
        tw("SS", vs3)
        bank_w(2); bank_w(3)
        v2 = mm(pb[2][:, 0:ABLK], onesrow[:], mrow[:], True, True, inc=True)
        v3 = mm(pb[3][:, 0:ABLK], onesrow[:], rrow[:], True, True, inc=True)
        B.bank_done[2] = ("ST", v2); B.bank_done[3] = ("ST", v3)
        consume("scalar", 2); consume("scalar", 3)
        sca("activation", out=mbc[:], in_=pb[2][:, 0:ABLK], func=AF.Copy)
        vsb = sca("activation", inc=True, out=rbc[:], in_=pb[3][:, 0:ABLK], func=AF.Copy)
        B.bank_free[2] = ("SS", vsb); B.bank_free[3] = ("SS", vsb)
        for k in range(8):
            sl = k % 2
            B.wait_pair("gpsimd", slot_free[sl])
            vx = dma("gpsimd", f"SX{sl}", xk[:, sl * ABLK:(sl + 1) * ABLK],
                     xotD[k * 128:(k + 1) * 128, c0:c0 + ABLK])
            B.wait("vector", f"SX{sl}", vx)
            B.wait("vector", "SS", vsb)
            col = k * TOWN + c0
            vec("tensor_sub", out=tmpA[:], in0=xk[:, sl * ABLK:(sl + 1) * ABLK], in1=mbc[:])
            vec("tensor_mul", out=xn_own[:, col:col + ABLK], in0=tmpA[:], in1=rbc[:])
            vv = vec("tensor_copy", inc=True, out=acc[:, col:col + ABLK],
                     in_=xk[:, sl * ABLK:(sl + 1) * ABLK])
            slot_free[sl] = ("SV", vv)
    xn_ready = B.cnt["SV"]

    tw("SV", xn_ready)
    for blk in range(4):
      c0b = blk * ABLK
      bank_w(0)
      for k in range(8):
          v = mm(pb[0][0:8, 0:ABLK], router[:, k * 8:(k + 1) * 8],
                 xn_own[:, k * TOWN + c0b: k * TOWN + c0b + ABLK],
                 start=(k == 0), stop=(k == 7), inc=(k == 7))
      B.bank_done[0] = ("ST", v)
      consume("scalar", 0)
      vsig = sca("activation", inc=True, out=probs_fm[:],
                 in_=pb[0][0:8, 0:ABLK], func=AF.Sigmoid, bias=rbias[:])
      B.bank_free[0] = ("SS", vsig)
      for tts in range(4):
        tt = blk * 4 + tts
        c0 = tts * 128
        b6, b7 = (2, 3) if tt % 2 == 0 else (3, 2)
        B.wait("tensor", "SS", vsig)
        bank_w(b6)
        v = tp(pb[b6][:, 0:8], probs_fm[:, c0:c0 + 128], id8f[:], inc=True)
        B.bank_done[b6] = ("ST", v)
        consume("vector", b6)
        vv = vec("tensor_copy", inc=True, out=ptm[:], in_=pb[b6][:, 0:8])
        B.bank_free[b6] = ("SV", vv)
        vv = vec("reduce_max", inc=True, out=p1col[:], in_=ptm[:], axis=X)
        B.wait("vector", "SV", vv)
        vec("tensor_single_scalar", out=eqm[:], in_=ptm[:], scalar=p1col[:], op=OP.is_equal)
        vec("scalar_tensor_tensor", out=eqm[:], in0=eqm[:], scalar=-1e30, in1=ptm[:],
            op0=OP.mult, op1=OP.add)
        vv = vec("reduce_max", inc=True, out=p2col[:], in_=eqm[:], axis=X)
        vec("tensor_add", out=s12[:], in0=p1col[:], in1=p2col[:])
        vv2 = vec("reciprocal", inc=True, out=rs12[:], in_=s12[:])
        B.wait("vector", "SV", vv2)
        vec("tensor_single_scalar", out=wtm[:], in_=ptm[:], scalar=p2col[:], op=OP.is_ge)
        vec("tensor_mul", out=wtm[:], in0=wtm[:], in1=ptm[:])
        vw = vec("tensor_scalar_mul", inc=True, out=wtm[:], in0=wtm[:], scalar1=rs12[:])
        tw("SV", vw)
        bank_w(b7)
        v = tp(pb[b7][0:8, 0:128], wtm[:], id128f[:], inc=True)
        B.bank_done[b7] = ("ST", v)
        consume("vector", b7)
        vv = vec("tensor_copy", inc=True, out=wT[:, tt * 128:tt * 128 + 128], in_=pb[b7][0:8, 0:128])
        B.bank_free[b7] = ("SV", vv)
    wt_ready = B.cnt["SV"]

    ring = {"g": {0: None, 1: None}, "u": {0: None, 1: None}, "d": {0: None, 1: None}}
    for e in range(9):
        shared = (e == 8)
        srcg = sgD if shared else wgD[e]
        srcu = suD if shared else wuD[e]
        srcda = sdaD if shared else wdaD[e]
        srcdb = sdbD if shared else wdbD[e]
        for blk in range(4):
            c0 = blk * ABLK
            wbc_ready = None
            if not shared:
                tw("SV", wt_ready)
                bank_w(1)
                v = mm(pb[1][:, 0:ABLK], sel8[:, e * 128:(e + 1) * 128],
                       wT[:, c0:c0 + ABLK], True, True, inc=True)
                B.bank_done[1] = ("ST", v)
                consume("vector", 1)
                wbc_ready = vec("tensor_copy", inc=True, out=wbc[:], in_=pb[1][:, 0:ABLK])
                B.bank_free[1] = ("SV", wbc_ready)
            hmarks = [None] * 16
            for m in range(16):
                sl = m % 2
                B.wait_pair("sync", ring["g"][sl])
                vwg = dma("sync", f"SG{sl}", rgw[:, sl * 1024:(sl + 1) * 1024], srcg[m])
                B.wait_pair("sync", ring["u"][sl])
                vwu = dma("sync", f"SU{sl}", rgu[:, sl * 1024:(sl + 1) * 1024], srcu[m])
                bg, bu = 4 + sl, 2 + sl
                tw(f"SG{sl}", vwg)
                bank_w(bg)
                for k in range(8):
                    vG = mm(pb[bg][:, 0:ABLK],
                            rgw[:, sl * 1024 + k * 128: sl * 1024 + (k + 1) * 128],
                            xn_own[:, k * TOWN + c0: k * TOWN + c0 + ABLK],
                            start=(k == 0), stop=(k == 7), inc=(k == 7))
                ring["g"][sl] = ("ST", vG)
                B.bank_done[bg] = ("ST", vG)
                tw(f"SU{sl}", vwu)
                bank_w(bu)
                for k in range(8):
                    vU = mm(pb[bu][:, 0:ABLK],
                            rgu[:, sl * 1024 + k * 128: sl * 1024 + (k + 1) * 128],
                            xn_own[:, k * TOWN + c0: k * TOWN + c0 + ABLK],
                            start=(k == 0), stop=(k == 7), inc=(k == 7))
                ring["u"][sl] = ("ST", vU)
                B.bank_done[bu] = ("ST", vU)
                consume("scalar", bg)
                vsil = sca("activation", inc=True, out=silu_sb[:, sl * ABLK:(sl + 1) * ABLK],
                           in_=pb[bg][:, 0:ABLK], func=AF.Silu)
                B.bank_free[bg] = ("SS", vsil)
                B.wait("vector", "SS", vsil)
                consume("vector", bu)
                hc = m * ABLK
                if shared:
                    vh = vec("tensor_mul", inc=True, out=h_sb[:, hc:hc + ABLK],
                             in0=silu_sb[:, sl * ABLK:(sl + 1) * ABLK],
                             in1=pb[bu][:, 0:ABLK])
                else:
                    B.wait("vector", "SV", wbc_ready)
                    vec("tensor_mul", out=h_sb[:, hc:hc + ABLK],
                        in0=silu_sb[:, sl * ABLK:(sl + 1) * ABLK], in1=pb[bu][:, 0:ABLK])
                    vh = vec("tensor_mul", inc=True, out=h_sb[:, hc:hc + ABLK],
                             in0=h_sb[:, hc:hc + ABLK], in1=wbc[:])
                B.bank_free[bu] = ("SV", vh)
                hmarks[m] = vh
            vstop = {}
            for i in range(16):
                sl = i % 2
                B.wait_pair("sync", ring["d"][sl])
                vwd = dma("sync", f"SD{sl}", rgd[:, sl * 1024:sl * 1024 + 768], srcda[i])
                tw(f"SD{sl}", vwd)
                tw("SV", hmarks[i])
                if i == 0:
                    for bbb in range(6):
                        bank_w(bbb)
                for dmj in range(6):
                    v = mm(pb[dmj][:, 0:ABLK],
                           rgd[:, sl * 1024 + dmj * 128: sl * 1024 + (dmj + 1) * 128],
                           h_sb[:, i * ABLK:(i + 1) * ABLK],
                           start=(i == 0), stop=(i == 15), inc=(dmj == 5 or i == 15))
                    if i == 15:
                        vstop[dmj] = v
                ring["d"][sl] = ("ST", v)
            for dmj in range(6):
                B.bank_done[dmj] = ("ST", vstop[dmj])
                consume("vector", dmj)
                col = dmj * TOWN + c0
                vv = vec("tensor_add", inc=True, out=acc[:, col:col + ABLK],
                         in0=pb[dmj][:, 0:ABLK], in1=acc[:, col:col + ABLK])
                B.bank_free[dmj] = ("SV", vv)
            for i in range(16):
                sl = i % 2
                B.wait_pair("sync", ring["d"][sl])
                vwd = dma("sync", f"SD{sl}", rgd[:, sl * 1024 + 768:sl * 1024 + 1024], srcdb[i])
                tw(f"SD{sl}", vwd)
                if i == 0:
                    bank_w(0); bank_w(1)
                for dmj in range(2):
                    v = mm(pb[dmj][:, 0:ABLK],
                           rgd[:, sl * 1024 + 768 + dmj * 128: sl * 1024 + 768 + (dmj + 1) * 128],
                           h_sb[:, i * ABLK:(i + 1) * ABLK],
                           start=(i == 0), stop=(i == 15), inc=(dmj == 1 or i == 15))
                    if i == 15:
                        vstop[dmj] = v
                ring["d"][sl] = ("ST", v)
            for dmj in range(2):
                B.bank_done[dmj] = ("ST", vstop[dmj])
                consume("vector", dmj)
                col = (6 + dmj) * TOWN + c0
                vv = vec("tensor_add", inc=True, out=acc[:, col:col + ABLK],
                         in0=pb[dmj][:, 0:ABLK], in1=acc[:, col:col + ABLK])
                B.bank_free[dmj] = ("SV", vv)

    B.wait("tensor", "ST", B.cnt["ST"])
    pa_t = B.inc("ST")
    B.emit("tensor", lambda E, sh=sems["ST"]: E.nop().then_inc(sh, 1))
    B.wait("vector", "ST", pa_t)
    vec("memset", ap=xbcn[:], constant=0.0)
    vec("memset", ap=r_sb[:], constant=0.0)
    vec("memset", ap=rb16[:], constant=0.0)
    vbar = vec("memset", inc=True, ap=carry[:], constant=0.0)
    B.wait("scalar", "SV", vbar)
    B.wait("tensor", "SV", vbar)
    B.wait("sync", "SV", vbar)
    B.wait("gpsimd", "SV", vbar)

    slot_free = {i: None for i in range(4)}
    woutring = {0: None, 1: None}
    outst = {mo: None for mo in range(8)}
    va0 = vlnn = vconv = None

    for blk in range(NBB):
        c0 = blk * BBLK
        for k in range(8):
            s0, sb_ = k % 2, 2 + k % 2
            B.wait_pair("gpsimd", slot_free[s0])
            vx0 = dma("gpsimd", f"SX{s0}", xk[:, s0 * BBLK:(s0 + 1) * BBLK],
                      x0tD[k * 128:(k + 1) * 128, c0:c0 + BBLK])
            B.wait_pair("gpsimd", slot_free[sb_])
            vxb = dma("gpsimd", f"SX{sb_}", xk[:, sb_ * BBLK:(sb_ + 1) * BBLK],
                      xbtD[k * 128:(k + 1) * 128, c0:c0 + BBLK])
            B.wait("scalar", f"SX{s0}", vx0)
            vq0 = sca("activation", inc=True, out=sq[:, s0 * BBLK:(s0 + 1) * BBLK],
                      in_=xk[:, s0 * BBLK:(s0 + 1) * BBLK], func=AF.Square)
            B.wait("scalar", f"SX{sb_}", vxb)
            vqb = sca("activation", inc=True, out=sq[:, sb_ * BBLK:(sb_ + 1) * BBLK],
                      in_=xk[:, sb_ * BBLK:(sb_ + 1) * BBLK], func=AF.Square)
            tw(f"SX{s0}", vx0)
            tw(f"SX{sb_}", vxb)
            if k == 0:
                for bbb in (0, 1, 2, 3, 5):
                    bank_w(bbb)
            mm(pb[0][0:1, 0:BBLK], onescol[:], xk[:, s0 * BBLK:(s0 + 1) * BBLK],
               start=(k == 0), stop=(k == 7))
            B.wait("tensor", "SS", vq0)
            v1 = mm(pb[1][0:1, 0:BBLK], onescol[:], sq[:, s0 * BBLK:(s0 + 1) * BBLK],
                    start=(k == 0), stop=(k == 7), inc=True)
            slot_free[s0] = ("ST", v1)
            mm(pb[2][0:1, 0:BBLK], onescol[:], xk[:, sb_ * BBLK:(sb_ + 1) * BBLK],
               start=(k == 0), stop=(k == 7))
            B.wait("tensor", "SS", vqb)
            mm(pb[3][0:1, 0:BBLK], onescol[:], sq[:, sb_ * BBLK:(sb_ + 1) * BBLK],
               start=(k == 0), stop=(k == 7))
            vd = mm(pb[5][0:16, 0:BBLK], windt[:, k * 16:(k + 1) * 16],
                    xk[:, sb_ * BBLK:(sb_ + 1) * BBLK],
                    start=(k == 0), stop=False, inc=True)
            slot_free[sb_] = ("ST", vd)
        vstats = B.cnt["ST"]
        for bq in range(4):
            B.bank_done[bq] = ("ST", vstats)
            consume("scalar", bq)
        sca("activation", out=mrow[:, 0:BBLK], in_=pb[0][0:1, 0:BBLK], func=AF.Copy,
            scale=sconst[0:1, 1:2])
        sca("activation", out=msqrow[:, 0:BBLK], in_=pb[1][0:1, 0:BBLK], func=AF.Copy,
            scale=sconst[0:1, 1:2])
        sca("activation", out=negmb[:], in_=pb[2][0:1, 0:BBLK], func=AF.Copy,
            scale=sconst[0:1, 2:3])
        vs0 = sca("activation", inc=True, out=msqb[:], in_=pb[3][0:1, 0:BBLK],
                  func=AF.Copy, scale=sconst[0:1, 1:2])
        for bq in range(4):
            B.bank_free[bq] = ("SS", vs0)
        B.wait("scalar", "SS", vs0)
        sca("activation", out=m2row[:, 0:BBLK], in_=mrow[:, 0:BBLK], func=AF.Square)
        vs1 = sca("activation", inc=True, out=mb2row[:], in_=negmb[:], func=AF.Square)
        B.wait("vector", "SS", vs1)
        vec("tensor_sub", out=varrow[:, 0:BBLK], in0=msqrow[:, 0:BBLK], in1=m2row[:, 0:BBLK])
        vv = vec("tensor_sub", inc=True, out=varb[:], in0=msqb[:], in1=mb2row[:])
        B.wait("scalar", "SV", vv)
        sca("activation", out=sdrow[:, 0:BBLK], in_=varrow[:, 0:BBLK], func=AF.Ln,
            bias=sconst[0:1, 0:1])
        vln2 = sca("activation", inc=True, out=sdb[:], in_=varb[:], func=AF.Ln,
                   bias=sconst[0:1, 0:1])
        B.wait("scalar", "SS", vln2)
        sca("activation", out=rrow[:, 0:BBLK], in_=sdrow[:, 0:BBLK], func=AF.Exp,
            scale=sconst[0:1, 5:6])
        vrec = sca("activation", inc=True, out=rbrow[:], in_=sdb[:], func=AF.Exp,
                   scale=sconst[0:1, 5:6])
        B.wait("tensor", "SS", vs0)
        vd = mm(pb[5][0:16, 0:BBLK], w1dt[:], negmb[:], start=False, stop=True, inc=True)
        B.bank_done[5] = ("ST", vd)
        tw("SS", vrec)
        bank_w(2); bank_w(3); bank_w(4)
        v2 = mm(pb[2][:, 0:BBLK], onesrow[:], mrow[:, 0:BBLK], True, True, inc=True)
        v3 = mm(pb[3][:, 0:BBLK], onesrow[:], rrow[:, 0:BBLK], True, True, inc=True)
        v4 = mm(pb[4][:, 0:BBLK], onesrow[:], rbrow[:], True, True, inc=True)
        B.bank_done[2] = ("ST", v2); B.bank_done[3] = ("ST", v3); B.bank_done[4] = ("ST", v4)
        consume("scalar", 2); consume("scalar", 3); consume("scalar", 4)
        sca("activation", out=mbc[:, 0:BBLK], in_=pb[2][:, 0:BBLK], func=AF.Copy)
        sca("activation", out=rbc[:, 0:BBLK], in_=pb[3][:, 0:BBLK], func=AF.Copy)
        vs3 = sca("activation", inc=True, out=rbbc[:], in_=pb[4][:, 0:BBLK], func=AF.Copy)
        B.bank_free[2] = ("SS", vs3); B.bank_free[3] = ("SS", vs3); B.bank_free[4] = ("SS", vs3)
        consume("vector", 5)
        B.wait("vector", "SS", vs3)
        vv = vec("tensor_mul", inc=True, out=dtrb_sb[:], in0=pb[5][0:16, 0:BBLK],
                 in1=rbbc[0:16, :])
        B.bank_free[5] = ("SV", vv)
        B.wait("scalar", "SV", vv)
        vsp1 = sca("activation", inc=True, out=eab_sb[:], in_=dtrb_sb[:], func=AF.Exp,
                   bias=dtbias[:])
        B.wait("scalar", "SS", vsp1)
        vsp2 = sca("activation", inc=True, out=dtb_sb[:], in_=eab_sb[:], func=AF.Ln,
                   bias=sconst[0:16, 4:5])
        B.wait("scalar", "SS", vsp2)
        vse = sca("activation", inc=True, out=eab_sb[:], in_=dtb_sb[:], func=AF.Exp,
                  scale=aparam[:])
        B.wait("vector", "SS", vse)
        vsc = vscan(norm_sb[:, 0:128], eab_sb[:, 0:128], ones16[:], carry[:])
        B.wait("vector", "SV", vsc)
        vsc2 = vscan(norm_sb[:, 128:256], eab_sb[:, 128:256], ones16[:],
                     norm_sb[:, 127:128])
        B.wait("vector", "SV", vsc2)
        vcar = vec("tensor_copy", inc=True, out=carry[:], in_=norm_sb[:, 255:256])
        B.wait("scalar", "SV", vcar)
        vlnn = sca("activation", inc=True, out=lnn_sb[:], in_=norm_sb[:], func=AF.Ln)
        for k in range(8):
            s0 = k % 2
            B.wait_pair("gpsimd", slot_free[s0])
            vx0 = dma("gpsimd", f"SX{s0}", xk[:, s0 * BBLK:(s0 + 1) * BBLK],
                      x0tD[k * 128:(k + 1) * 128, c0:c0 + BBLK])
            B.wait("vector", f"SX{s0}", vx0)
            B.wait("vector", "SS", vs3)
            vec("tensor_sub", out=tmpA[:, 0:BBLK], in0=xk[:, s0 * BBLK:(s0 + 1) * BBLK],
                in1=mbc[:, 0:BBLK])
            vv = vec("tensor_mul", inc=True, out=xn0[:, k * BBLK:(k + 1) * BBLK],
                     in0=tmpA[:, 0:BBLK], in1=rbc[:, 0:BBLK])
            slot_free[s0] = ("SV", vv)
        vxn0 = B.cnt["SV"]
        tw("SV", vxn0)
        bank_w(5)
        for k in range(8):
            vd0 = mm(pb[5][0:16, 0:BBLK], windt[:, k * 16:(k + 1) * 16],
                     xn0[:, k * BBLK:(k + 1) * BBLK],
                     start=(k == 0), stop=(k == 7), inc=(k == 7))
        B.bank_done[5] = ("ST", vd0)
        consume("scalar", 5)
        vs5 = sca("activation", inc=True, out=dtr0_sb[:], in_=pb[5][0:16, 0:BBLK],
                  func=AF.Copy)
        B.bank_free[5] = ("SS", vs5)
        B.wait("scalar", "SS", vs5)
        vs6 = sca("activation", inc=True, out=norm_sb[:], in_=dtr0_sb[:], func=AF.Exp,
                  bias=dtbias[:])
        B.wait("scalar", "SS", vs6)
        vsp = sca("activation", inc=True, out=dt0_sb[:], in_=norm_sb[:],
                  func=AF.Ln, bias=sconst[0:16, 4:5])
        B.wait("vector", "SS", vsp)
        va0 = vec("tensor_scalar_mul", inc=True, out=a0_sb[:], in0=dt0_sb[:],
                  scalar1=aparam[:])
        for c in range(2):
            ch0 = c * CH
            B.wait("vector", "SV", va0)
            vsc = vscan(cuma[:], ones16[:], a0_sb[:, ch0:ch0 + CH], 0.0)
            B.wait("vector", "SS", vlnn)
            vpsi = vec("tensor_sub", inc=True, out=psiin[:], in0=cuma[:],
                       in1=lnn_sb[:, ch0:ch0 + CH])
            sca("activation", out=fexp[:], in_=cuma[:], func=AF.Exp, scale=sconst[0:16, 3:4])
            sca("activation", out=dexp[:], in_=cuma[:], func=AF.Exp, scale=sconst[0:16, 3:4],
                bias=cuma[:, CH - 1:CH])
            sca("activation", out=lam[:], in_=cuma[:, CH - 1:CH], func=AF.Exp)
            vsx = sca("activation", inc=True, out=psie[:], in_=psiin[:], func=AF.Exp)
            vls = vec("tensor_scalar_mul", inc=True, out=lamsel[:], in0=selmask[:],
                      scalar1=lam[:])
            tw("SS", vsx); tw("SV", vls)
            bank_w(4)
            tp(pb[4][:, 0:16], fexp[:], id16f[:])
            tp(pb[4][:, 16:32], dexp[:], id16f[:])
            tp(pb[4][:, 32:48], psie[:], id16f[:])
            vt4 = mm(pb[4][:, 48:56], lsel[:], lamsel[:], True, True, inc=True)
            B.bank_done[4] = ("ST", vt4)
            consume("vector", 4)
            vec("tensor_copy", out=ft_sb[:, c * 16:(c + 1) * 16], in_=pb[4][:, 0:16])
            vec("tensor_copy", out=dtt_sb[:, c * 16:(c + 1) * 16], in_=pb[4][:, 16:32])
            vec("tensor_copy", out=psit_sb[:, c * 16:(c + 1) * 16], in_=pb[4][:, 32:48])
            vsm = vec("tensor_copy", inc=True, out=lamx_sb[:, c * 8:(c + 1) * 8],
                      in_=pb[4][:, 48:56])
            B.bank_free[4] = ("SV", vsm)
        for mt in range(24):
            sl = mt % 2
            B.wait_pair("sync", ring["g"][sl])
            vw = dma("sync", f"SG{sl}", rgw[:, sl * 1024:(sl + 1) * 1024], wingD[mt])
            bk = sl
            tw(f"SG{sl}", vw)
            bank_w(bk)
            for k in range(8):
                vp = mm(pb[bk][:, 0:BBLK],
                        rgw[:, sl * 1024 + k * 128: sl * 1024 + (k + 1) * 128],
                        xn0[:, k * BBLK:(k + 1) * BBLK],
                        start=(k == 0), stop=(k == 7), inc=(k == 7))
            ring["g"][sl] = ("ST", vp)
            B.bank_done[bk] = ("ST", vp)
            consume("scalar", bk)
            col = mt * (BBLK + 3) + 3
            vpe = sca("activation", inc=True, out=xbcn[:, col:col + BBLK],
                      in_=pb[bk][:, 0:BBLK], func=AF.Copy)
            B.bank_free[bk] = ("SS", vpe)
        vproj = B.cnt["SS"]
        B.wait("vector", "SS", vproj)
        conv_mark = {}
        ftorder = list(range(8, 24)) + list(range(8))
        for grp in range(0, 24, 8):
            fts = ftorder[grp:grp + 8]
            for kk in range(4):
                for ft in fts:
                    cin = ft * (BBLK + 3)
                    cout = ft * BBLK
                    if kk == 0:
                        v = vec("tensor_scalar", out=convo[:, cout:cout + BBLK],
                                in0=xbcn[:, cin:cin + BBLK],
                                scalar1=convw[:, ft * 4:ft * 4 + 1],
                                scalar2=convb[:, ft:ft + 1], op0=OP.mult, op1=OP.add)
                    else:
                        v = vec("scalar_tensor_tensor", out=convo[:, cout:cout + BBLK],
                                in0=xbcn[:, cin + kk:cin + kk + BBLK],
                                scalar=convw[:, ft * 4 + kk:ft * 4 + kk + 1],
                                in1=convo[:, cout:cout + BBLK], op0=OP.mult, op1=OP.add)
                    if kk == 3:
                        conv_mark[ft] = ("SV", v)
        vconv = vec("tensor_copy", inc=True,
                    out=xbcn.ap().rearrange("p (f c) -> p f c", f=24)[:, :, 0:3],
                    in_=xbcn.ap().rearrange("p (f c) -> p f c", f=24)[:, :, BBLK:BBLK + 3])
        for c in range(2):
            ch0 = c * CH
            for i in range(8):
                bt = 8 + i % 2
                B.wait_pair("tensor", conv_mark[16 + i])
                bank_w(bt)
                cc = (16 + i) * BBLK + ch0
                v = tp(pbB[i % 2][:, 0:128], convo[:, cc:cc + CH], id128b[:], inc=True)
                B.bank_done[bt] = ("ST", v)
                consume("vector", bt)
                vv = vec("tensor_copy", inc=True, out=xt_sb[:, i * 128:(i + 1) * 128],
                         in_=pbB[i % 2][:, 0:128])
                B.bank_free[bt] = ("SV", vv)
            for i in range(8):
                bt = 8 + i % 2
                B.wait_pair("tensor", conv_mark[8 + i])
                bank_w(bt)
                cc = (8 + i) * BBLK + ch0
                v = tp(pbB[i % 2][:, 0:128], convo[:, cc:cc + CH], id128b[:], inc=True)
                B.bank_done[bt] = ("ST", v)
                consume("vector", bt)
                vec("tensor_scalar_mul", out=bdt_sb[:, i * 128:i * 128 + 64],
                    in0=pbB[i % 2][:, 0:64], scalar1=dtt_sb[:, c * 16 + 2 * i:c * 16 + 2 * i + 1])
                vv = vec("tensor_scalar_mul", inc=True,
                         out=bdt_sb[:, i * 128 + 64:(i + 1) * 128],
                         in0=pbB[i % 2][:, 64:128], scalar1=dtt_sb[:, c * 16 + 2 * i + 1:c * 16 + 2 * i + 2])
                B.bank_free[bt] = ("SV", vv)
            vxb_ready = B.cnt["SV"]
            tw("SV", vxb_ready)
            vgm_all = None
            for h in range(16):
                j, hh = h // 2, h % 2
                pbase = hh * 64
                bG = 4 + h % 2
                ftB = (8 + j) * BBLK + ch0
                ftC = j * BBLK + ch0
                B.wait_pair("tensor", conv_mark[j])
                bank_w(bG)
                vg = mm(pb[bG][:, 0:128],
                        convo[pbase:pbase + 64, ftB:ftB + CH],
                        convo[pbase:pbase + 64, ftC:ftC + CH],
                        True, True, inc=True)
                B.bank_done[bG] = ("ST", vg)
                consume("vector", bG)
                vgm_all = vec("scalar_tensor_tensor", inc=True,
                              out=gmt_sb[:, h * 128:(h + 1) * 128],
                              in0=pb[bG][:, 0:128], scalar=ft_sb[:, c * 16 + h:c * 16 + h + 1],
                              in1=trimask[:], op0=OP.mult, op1=OP.mult)
                B.bank_free[bG] = ("SV", vgm_all)
            tw("SV", vgm_all)
            for j in range(8):
                bY = 2 + j % 2
                bS = 0 + j % 2
                vy = vs_ = None
                for hh in range(2):
                    h = 2 * j + hh
                    pbase = hh * 64
                    ftC = j * BBLK + ch0
                    if hh == 0:
                        bank_w(bY)
                        bank_w(bS)
                    mm(pb[bY][:, pbase:pbase + 64], gmt_sb[:, h * 128:(h + 1) * 128],
                       xt_sb[:, j * 128 + pbase: j * 128 + pbase + 64],
                       start=True, stop=False)
                    vy = mm(pb[bY][:, pbase:pbase + 64],
                            convo[pbase:pbase + 64, ftC:ftC + CH],
                            rb16[pbase:pbase + 64, j * 64:(j + 1) * 64],
                            start=False, stop=True, inc=True)
                    vs_ = mm(pb[bS][0:64, pbase:pbase + 64],
                             bdt_sb[:, j * 128 + pbase: j * 128 + pbase + 64],
                             xt_sb[:, j * 128 + pbase: j * 128 + pbase + 64],
                             True, True, inc=True)
                B.bank_done[bY] = ("ST", vy)
                B.bank_done[bS] = ("ST", vs_)
                consume("vector", bY)
                for hh in range(2):
                    h = 2 * j + hh
                    vv = vec("tensor_scalar_mul", inc=(hh == 1),
                             out=ztm[:, j * 128 + hh * 64: j * 128 + (hh + 1) * 64],
                             in0=pb[bY][:, hh * 64:(hh + 1) * 64],
                             scalar1=psit_sb[:, c * 16 + h:c * 16 + h + 1])
                B.bank_free[bY] = ("SV", vv)
                consume("vector", bS)
                for hh in range(2):
                    vv = vec("scalar_tensor_tensor", inc=(hh == 1),
                             out=r_sb[hh * 64:(hh + 1) * 64, j * 64:(j + 1) * 64],
                             in0=r_sb[hh * 64:(hh + 1) * 64, j * 64:(j + 1) * 64],
                             scalar=lamx_sb[hh * 64:(hh + 1) * 64, c * 8 + j:c * 8 + j + 1],
                             in1=pb[bS][0:64, hh * 64:(hh + 1) * 64],
                             op0=OP.mult, op1=OP.add)
                B.bank_free[bS] = ("SV", vv)
            vrb = vec("tensor_copy", inc=True, out=rb16[:], in_=r_sb[:])
            tw("SV", vrb)
            for i in range(8):
                bt = 8 + i % 2
                bank_w(bt)
                v = tp(pbB[i % 2][:, 0:128], ztm[:, i * 128:(i + 1) * 128], id128b[:], inc=True)
                B.bank_done[bt] = ("ST", v)
                consume("vector", bt)
                vv = vec("tensor_copy", inc=True,
                         out=zfm[:, i * BBLK + ch0: i * BBLK + ch0 + CH],
                         in_=pbB[i % 2][:, 0:128])
                B.bank_free[bt] = ("SV", vv)
        vzfm = B.cnt["SV"]
        tw("SV", vzfm)
        for mo in range(8):
            sl = mo % 2
            B.wait_pair("sync", woutring[sl])
            vw = dma("sync", f"SU{sl}", rgu[:, sl * 1024:(sl + 1) * 1024], woutD[mo])
            bk = sl
            tw(f"SU{sl}", vw)
            bank_w(bk)
            for ko in range(8):
                vwo = mm(pb[bk][:, 0:BBLK],
                         rgu[:, sl * 1024 + ko * 128: sl * 1024 + (ko + 1) * 128],
                         zfm[:, ko * BBLK:(ko + 1) * BBLK],
                         start=(ko == 0), stop=(ko == 7), inc=(ko == 7))
            woutring[sl] = ("ST", vwo)
            B.bank_done[bk] = ("ST", vwo)
            consume("vector", bk)
            acol = mo * TOWN + (blk % 8) * BBLK
            B.wait_pair("vector", outst[mo])
            vv = vec("tensor_add", inc=True, out=outc_sb[:, mo * BBLK:(mo + 1) * BBLK],
                     in0=pb[bk][:, 0:BBLK], in1=acc[:, acol:acol + BBLK])
            B.bank_free[bk] = ("SV", vv)
            B.wait("gpsimd", "SV", vv)
            vso = dma("gpsimd", f"SO{mo}", outcD[mo * 128:(mo + 1) * 128, c0:c0 + BBLK],
                      outc_sb[:, mo * BBLK:(mo + 1) * BBLK])
            outst[mo] = (f"SO{mo}", vso)

    for mo in range(8):
        B.wait("gpsimd", f"SO{mo}", B.cnt[f"SO{mo}"])

    import contextlib
    with nc.allow_low_precision(reason="bf16 residual accumulator; within measured tolerance"), nc.Block() as block:
        def runner(lst):
            def go(E, lst=lst):
                for fn in lst:
                    fn(E)
            return go
        block.tensor(runner(B.prog["tensor"]))
        block.vector(runner(B.prog["vector"]))
        block.scalar(runner(B.prog["scalar"]))
        block.sync(runner(B.prog["sync"]))
        block.gpsimd(runner(B.prog["gpsimd"]))
    return nc


def _host_inputs(inputs):
    f32 = np.float32
    x = np.asarray(inputs["L0_attn_x"], f32)
    W_in = np.asarray(inputs["L0_attn_W_in"], f32)
    conv_w = np.asarray(inputs["L0_attn_conv_w"], f32)
    conv_b = np.asarray(inputs["L0_attn_conv_b"], f32)
    A_param = np.asarray(inputs["L0_attn_A_param"], f32)
    dt_bias = np.asarray(inputs["L0_attn_dt_bias"], f32)
    W_out = np.asarray(inputs["L0_attn_W_out"], f32)
    sg = np.asarray(inputs["L0_ffn_sg_w"], f32)
    su = np.asarray(inputs["L0_ffn_su_w"], f32)
    sd = np.asarray(inputs["L0_ffn_sd_w"], f32)
    rw = np.asarray(inputs["L0_ffn_router_w"], f32)
    rb = np.asarray(inputs["L0_ffn_routing_bias"], f32)
    wg = np.asarray(inputs["L0_ffn_wg"], f32)
    wu = np.asarray(inputs["L0_ffn_wu"], f32)
    wd = np.asarray(inputs["L0_ffn_wd"], f32)

    def up_tiles(w):
        return np.ascontiguousarray(
            w.T.reshape(8, 128, 16, 128).transpose(2, 1, 0, 3).reshape(16, 128, 1024))

    def down_tiles(w):
        return np.ascontiguousarray(w.T.reshape(16, 128, 1024))

    com = {}
    com["wing_d"] = np.ascontiguousarray(
        W_in[:3072].T.reshape(8, 128, 24, 128).transpose(2, 1, 0, 3).reshape(24, 128, 1024))
    com["windt_d"] = np.ascontiguousarray(
        W_in[3072:].T.reshape(8, 128, 16).transpose(1, 0, 2).reshape(128, 128))
    com["wout_d"] = np.ascontiguousarray(
        W_out.T.reshape(8, 128, 8, 128).transpose(2, 1, 0, 3).reshape(8, 128, 1024))
    com["router_d"] = np.ascontiguousarray(
        rw.T.reshape(8, 128, 8).transpose(1, 0, 2).reshape(128, 64))
    com["sg_d"] = up_tiles(sg)
    com["su_d"] = up_tiles(su)
    sdt = down_tiles(sd)
    com["sda_d"] = np.ascontiguousarray(sdt[:, :, 0:768])
    com["sdb_d"] = np.ascontiguousarray(sdt[:, :, 768:1024])
    com["wg_d"] = np.stack([up_tiles(wg[e]) for e in range(NE)])
    com["wu_d"] = np.stack([up_tiles(wu[e]) for e in range(NE)])
    wdt = np.stack([down_tiles(wd[e]) for e in range(NE)])
    com["wda_d"] = np.ascontiguousarray(wdt[:, :, :, 0:768])
    com["wdb_d"] = np.ascontiguousarray(wdt[:, :, :, 768:1024])
    com["convw_d"] = np.ascontiguousarray(
        conv_w.reshape(24, 128, 4).transpose(1, 0, 2).reshape(128, 96))
    com["convb_d"] = np.ascontiguousarray(conv_b.reshape(24, 128).T)
    com["aparam_d"] = np.ascontiguousarray(A_param[:, None])
    com["dtbias_d"] = np.ascontiguousarray(dt_bias[:, None])
    com["rbias_d"] = np.ascontiguousarray(rb[:, None])
    com["id128b_d"] = np.eye(128, dtype=ml_dtypes.bfloat16)
    com["id128f_d"] = np.eye(128, dtype=f32)
    com["id16f_d"] = np.eye(16, dtype=f32)
    com["id8f_d"] = np.eye(8, dtype=f32)
    com["trimask_d"] = np.triu(np.ones((128, 128), f32))
    com["onesrow_d"] = np.ones((1, 128), f32)
    com["onescol_d"] = np.ones((128, 1), f32)
    lsel = np.zeros((16, 128), f32)
    for h in range(16):
        if h % 2 == 0:
            lsel[h, :64] = 1.0
        else:
            lsel[h, 64:] = 1.0
    com["lsel_d"] = lsel
    selm = np.zeros((16, 8), f32)
    for h in range(16):
        selm[h, h // 2] = 1.0
    com["selmask_d"] = selm
    sel8 = np.zeros((8, 1024), f32)
    for e in range(8):
        sel8[e, e * 128:(e + 1) * 128] = 1.0
    com["sel8_d"] = sel8
    sc = np.zeros((128, 6), f32)
    sc[:, 0] = EPS; sc[:, 1] = 1.0 / D; sc[:, 2] = -1.0 / D; sc[:, 3] = -1.0
    sc[:, 4] = 1.0; sc[:, 5] = -0.5
    com["sconst_d"] = sc

    x0t = np.ascontiguousarray(x[0].T)
    in_maps = []
    for core in range(8):
        b, hf = core // 2, core % 2
        t0 = hf * TOWN
        m = dict(com)
        m["x0t_d"] = x0t
        m["xbt_d"] = x0t if b == 0 else np.ascontiguousarray(x[b].T)
        m["xot_d"] = np.ascontiguousarray(x[b, t0:t0 + TOWN].T)
        in_maps.append(m)
    return in_maps


def kernel(**inputs):
    in_maps = _host_inputs(inputs)
    nc = build_program()
    res = run_bass_kernel_spmd(nc, in_maps, core_ids=list(range(8)))
    out = np.empty((4, S, D), np.float32)
    for core in range(8):
        b, hf = core // 2, core % 2
        t0 = hf * TOWN
        oc = np.asarray(res.results[core]["outc"])
        out[b, t0:t0 + TOWN] = oc[:, t0:t0 + TOWN].T
    return out


# version 08
# speedup vs baseline: 1.1603x; 1.1603x over previous
import numpy as np
import ml_dtypes
import concourse.bass as bass
import concourse.mybir as mybir
from concourse.bass_utils import run_bass_kernel_spmd

F32 = mybir.dt.float32
F32R = mybir.dt.float32r
BF16 = mybir.dt.bfloat16
AF = mybir.ActivationFunctionType
OP = mybir.AluOpType

S = 4096
D = 1024
CH = 128
TOWN = 2048
ABLK = 512
BBLK = 256
NBB = S // BBLK
NE = 8
EPS = 1e-5

ENGINES = ("tensor", "vector", "scalar", "sync", "gpsimd")
SEMS = ("SW", "ST", "SS", "SV", "SO", "SX0", "SX1", "SX2", "SX3", "SG0", "SG1", "SU0", "SU1", "SD0", "SD1", "SO0", "SO1", "SO2", "SO3", "SO4", "SO5", "SO6", "SO7")


class Builder:
    def __init__(self):
        self.prog = {e: [] for e in ENGINES}
        self.cnt = {s: 0 for s in SEMS}
        self.lastwait = {}
        self.sems = None
        self.bank_free = {b: None for b in range(10)}
        self.bank_done = {b: None for b in range(10)}

    def emit(self, eng, fn):
        self.prog[eng].append(fn)

    def wait(self, eng, sem, val):
        if val is None or val <= 0:
            return
        key = (eng, sem)
        if self.lastwait.get(key, 0) >= val:
            return
        self.lastwait[key] = val
        self.emit(eng, lambda E, s=self.sems[sem], v=val: E.wait_ge(s, v))

    def wait_pair(self, eng, pair):
        if pair is not None:
            self.wait(eng, pair[0], pair[1])

    def inc(self, sem, n=1):
        self.cnt[sem] += n
        return self.cnt[sem]


def build_program():
    nc = bass.Bass("TRN2", target_bir_lowering=False)
    B = Builder()

    def din(name, shape, dtype=F32R):
        return nc.dram_tensor(name, list(shape), dtype, kind="ExternalInput").ap()

    x0tD = din("x0t_d", (D, S))
    xbtD = din("xbt_d", (D, S))
    xotD = din("xot_d", (D, TOWN))
    wingD = din("wing_d", (24, 128, 1024))
    windtD = din("windt_d", (128, 128))
    woutD = din("wout_d", (8, 128, 1024))
    routerD = din("router_d", (128, 64))
    sgD = din("sg_d", (16, 128, 1024))
    suD = din("su_d", (16, 128, 1024))
    sdaD = din("sda_d", (16, 128, 768))
    sdbD = din("sdb_d", (16, 128, 256))
    wgD = din("wg_d", (NE, 16, 128, 1024))
    wuD = din("wu_d", (NE, 16, 128, 1024))
    wdaD = din("wda_d", (NE, 16, 128, 768))
    wdbD = din("wdb_d", (NE, 16, 128, 256))
    convwD = din("convw_d", (128, 96), F32)
    convbD = din("convb_d", (128, 24), F32)
    aparamD = din("aparam_d", (16, 1), F32)
    dtbiasD = din("dtbias_d", (16, 1), F32)
    rbiasD = din("rbias_d", (8, 1), F32)
    id128bD = din("id128b_d", (128, 128), BF16)
    id128fD = din("id128f_d", (128, 128), F32)
    id16fD = din("id16f_d", (16, 16), F32)
    id8fD = din("id8f_d", (8, 8), F32)
    trimaskD = din("trimask_d", (128, 128), F32)
    onesrowD = din("onesrow_d", (1, 128), F32R)
    onescolD = din("onescol_d", (128, 1), F32R)
    lselD = din("lsel_d", (16, 128), F32)
    selmaskD = din("selmask_d", (16, 8), F32)
    sconstD = din("sconst_d", (128, 6), F32)
    sel8D = din("sel8_d", (8, 1024), BF16)
    outcD = nc.dram_tensor("outc", [D, S], F32, kind="ExternalOutput").ap()

    al = nc.alloc_sbuf_tensor
    at = nc.alloc_sbuf_tensor_at
    baseA, _ = nc.bump_sbuf(8 * TOWN * 4)
    baseH, _ = nc.bump_sbuf(38400)
    xnb = at("xnb", [128, 8 * TOWN], BF16, offset=baseA)
    xnr = at("xnr", [128, 8 * ABLK], F32R, offset=baseA + 32768)
    hb = at("hb", [128, 16 * ABLK], BF16, offset=baseH)
    o = baseA
    xbcn = at("xbcn", [128, 24 * (BBLK + 3)], BF16, offset=o); o += 24 * (BBLK + 3) * 2
    o = (o + 31) // 32 * 32
    convo = at("convo", [128, 24 * BBLK], BF16, offset=o); o += 24 * BBLK * 2
    o = (o + 31) // 32 * 32
    xt_sb = at("xt_sb", [128, 8 * 128], BF16, offset=o); o += 8 * 128 * 2
    bdt_sb = at("bdt_sb", [128, 8 * 128], BF16, offset=o); o += 8 * 128 * 2
    gmt_sb = at("gmt_sb", [128, 16 * 128], BF16, offset=o); o += 16 * 128 * 2
    ztm = at("ztm", [128, 1024], BF16, offset=o); o += 1024 * 2
    rb16 = at("rb16", [128, 8 * 64], BF16, offset=o); o += 8 * 64 * 2
    o = (o + 31) // 32 * 32
    negmb = at("negmb", [1, BBLK], F32R, offset=o); o += BBLK * 4
    msqb = at("msqb", [1, BBLK], F32, offset=o); o += BBLK * 4
    varb = at("varb", [1, BBLK], F32, offset=o); o += BBLK * 4
    sdb = at("sdb", [1, BBLK], F32, offset=o); o += BBLK * 4
    rbrow = at("rbrow", [1, BBLK], F32R, offset=o); o += BBLK * 4
    mb2row = at("mb2row", [1, BBLK], F32, offset=o); o += BBLK * 4
    rbbc = at("rbbc", [128, BBLK], F32, offset=o); o += BBLK * 4
    dtr0_sb = at("dtr0_sb", [16, BBLK], F32, offset=o); o += BBLK * 4
    dt0_sb = at("dt0_sb", [16, BBLK], F32, offset=o); o += BBLK * 4
    a0_sb = at("a0_sb", [16, BBLK], F32, offset=o); o += BBLK * 4
    dtrb_sb = at("dtrb_sb", [16, BBLK], F32, offset=o); o += BBLK * 4
    dtb_sb = at("dtb_sb", [16, BBLK], F32, offset=o); o += BBLK * 4
    eab_sb = at("eab_sb", [16, BBLK], F32, offset=o); o += BBLK * 4
    norm_sb = at("norm_sb", [16, BBLK], F32, offset=o); o += BBLK * 4
    lnn_sb = at("lnn_sb", [16, BBLK], F32, offset=o); o += BBLK * 4
    cuma = at("cuma", [16, CH], F32, offset=o); o += CH * 4
    fexp = at("fexp", [16, CH], F32, offset=o); o += CH * 4
    dexp = at("dexp", [16, CH], F32, offset=o); o += CH * 4
    psiin = at("psiin", [16, CH], F32, offset=o); o += CH * 4
    psie = at("psie", [16, CH], F32, offset=o); o += CH * 4
    lam = at("lam", [16, 1], F32, offset=o); o += 32
    lamsel = at("lamsel", [16, 8], F32, offset=o); o += 32
    assert o - baseA <= 8 * TOWN * 4
    o = baseH
    xn0 = at("xn0", [128, 8 * BBLK], F32R, offset=o); o += 8 * BBLK * 4
    xn0b = at("xn0b", [128, 8 * BBLK], BF16, offset=o); o += 8 * BBLK * 2
    zfm = at("zfm", [128, 8 * BBLK], BF16, offset=o); o += 8 * BBLK * 2
    outc_sb = at("outc_sb", [128, 8 * BBLK], F32, offset=o); o += 8 * BBLK * 4
    r_sb = at("r_sb", [128, 8 * 64], F32, offset=o); o += 8 * 64 * 4
    ft_sb = at("ft_sb", [128, 32], F32, offset=o); o += 32 * 4
    dtt_sb = at("dtt_sb", [128, 32], F32, offset=o); o += 32 * 4
    psit_sb = at("psit_sb", [128, 32], F32, offset=o); o += 32 * 4
    lamx_sb = at("lamx_sb", [128, 16], F32, offset=o); o += 16 * 4
    assert o - baseH <= 38400

    id128b = al("id128b", [128, 128], BF16)
    id128f = al("id128f", [128, 128], F32)
    id16f = al("id16f", [16, 16], F32)
    id8f = al("id8f", [8, 8], F32)
    trimask = al("trimask", [128, 128], F32)
    onesrow = al("onesrow", [1, 128], F32R)
    onescol = al("onescol", [128, 1], F32R)
    lsel = al("lsel", [16, 128], F32)
    selmask = al("selmask", [16, 8], F32)
    sel8 = al("sel8", [8, 1024], BF16)
    sconst = al("sconst", [128, 6], F32)
    convw = al("convw", [128, 96], F32)
    convb = al("convb", [128, 24], F32)
    aparam = al("aparam", [16, 1], F32)
    dtbias = al("dtbias", [16, 1], F32)
    rbias = al("rbias", [8, 1], F32)
    ones16 = al("ones16", [16, 128], F32)
    windt = al("windt", [128, 128], F32R)
    router = al("router", [128, 64], F32R)
    w1dt = al("w1dt", [1, 16], F32R)
    acc = al("acc", [128, 8 * TOWN], BF16)
    xk = al("xk", [128, 2 * ABLK], F32R)
    sq = al("sq", [128, 2 * ABLK], F32R)
    rgw = al("rgw", [128, 2 * 1024], F32R)
    rgu = al("rgu", [128, 2 * 1024], F32R)
    rgd = al("rgd", [128, 2 * 1024], F32R)
    rgwb = al("rgwb", [128, 2 * 1024], BF16)
    rgub = al("rgub", [128, 2 * 1024], BF16)
    rgdb = al("rgdb", [128, 2 * 1024], BF16)
    silu_sb = at("silu_sb", [128, 2 * ABLK], F32, offset=baseA + 49152)
    tmpA = al("tmpA", [128, ABLK], F32)
    wbc = at("wbc", [128, ABLK], F32, offset=baseA + 49152 + 4096)
    probs_fm = at("probs_fm", [8, ABLK], F32, offset=baseA + 49152 + 6144)
    ptm = al("ptm", [128, 8], F32)
    p1col = al("p1col", [128, 1], F32)
    p2col = al("p2col", [128, 1], F32)
    s12 = al("s12", [128, 1], F32)
    rs12 = al("rs12", [128, 1], F32)
    eqm = al("eqm", [128, 8], F32)
    wtm = al("wtm", [128, 8], F32)
    wT = al("wT", [8, TOWN], BF16)
    mrow = al("mrow", [1, ABLK], F32R)
    msqrow = al("msqrow", [1, ABLK], F32)
    varrow = al("varrow", [1, ABLK], F32)
    sdrow = al("sdrow", [1, ABLK], F32)
    rrow = al("rrow", [1, ABLK], F32R)
    m2row = al("m2row", [1, ABLK], F32)
    mbc = al("mbc", [128, ABLK], F32)
    rbc = al("rbc", [128, ABLK], F32)
    carry = al("carry", [16, 1], F32)

    pb = [nc.alloc_psum_tensor(f"pb{i}", [128, 512], F32) for i in range(6)]
    pbB = [nc.alloc_psum_tensor(f"pbB{i}", [128, 1024], BF16) for i in range(2)]
    sems = {s: nc.alloc_semaphore(s) for s in SEMS}
    B.sems = sems

    def dma(eng, sem, out_ap, in_ap):
        v = B.inc(sem, 16)
        B.emit(eng, lambda E, oa=out_ap, ia=in_ap, sh=sems[sem]:
               E.dma_start(out=oa, in_=ia).then_inc(sh, 16))
        return v

    def mm(out_ap, lhsT, rhs, start, stop, inc=False):
        v = B.inc("ST") if inc else None
        def fn(E, oa=out_ap, l=lhsT, r=rhs, st=start, sp=stop, sh=sems["ST"], vv=v):
            inst = E.matmul(oa, l, r, start=st, stop=sp)
            if vv is not None:
                inst.then_inc(sh, 1)
        B.emit("tensor", fn)
        return v

    def tp(out_ap, in_ap, ident, inc=False):
        v = B.inc("ST") if inc else None
        def fn(E, oa=out_ap, ia=in_ap, idn=ident, sh=sems["ST"], vv=v):
            inst = E.transpose(oa, ia, idn)
            if vv is not None:
                inst.then_inc(sh, 1)
        B.emit("tensor", fn)
        return v

    lastw = {}
    lastr = {}
    WKEYS = ("out", "ap", "accum_out")

    def _tname(v):
        t = getattr(v, "tensor", None)
        return getattr(t, "name", None) if t is not None else None

    def _autosync(eng, kw):
        rnames, wnames = [], []
        for k, v in kw.items():
            if isinstance(v, (int, float, str, bool)) or v is None:
                continue
            n = _tname(v)
            if n is None:
                continue
            (wnames if k in WKEYS else rnames).append(n)
        for n in rnames:
            if n in lastw:
                B.wait_pair(eng, lastw[n])
        for n in wnames:
            if n in lastw:
                B.wait_pair(eng, lastw[n])
            if n in lastr:
                B.wait_pair(eng, lastr[n])

    def _markw(eng, sem, kw, v):
        for k, val in kw.items():
            if isinstance(val, (int, float, str, bool)) or val is None:
                continue
            n = _tname(val)
            if n is None:
                continue
            if k in WKEYS:
                lastw[n] = (sem, v)
            else:
                lastr[n] = (sem, v)

    def vec(name, inc=False, notrack=False, **kw):
        if not notrack:
            _autosync("vector", kw)
        v = B.inc("SV")
        def fn(E, n=name, kw=kw, sh=sems["SV"], vv=v):
            getattr(E, n)(**kw).then_inc(sh, 1)
        B.emit("vector", fn)
        if not notrack:
            _markw("vector", "SV", kw, v)
        return v

    def vscan(out_ap, d0, d1, ini, inc=True):
        kw = {"out": out_ap, "d0": d0, "d1": d1, "ini": ini}
        _autosync("vector", kw)
        v = B.inc("SV")
        def fn(E, oa=out_ap, a=d0, b=d1, i=ini, sh=sems["SV"], vv=v):
            E.tensor_tensor_scan(oa, a, b, i, OP.mult, OP.add).then_inc(sh, 1)
        B.emit("vector", fn)
        _markw("vector", "SV", kw, v)
        return v

    def sca(name, inc=False, notrack=False, **kw):
        if not notrack:
            _autosync("scalar", kw)
        v = B.inc("SS")
        def fn(E, n=name, kw=kw, sh=sems["SS"], vv=v):
            getattr(E, n)(**kw).then_inc(sh, 1)
        B.emit("scalar", fn)
        if not notrack:
            _markw("scalar", "SS", kw, v)
        return v

    def tw(sem, val):
        B.wait("tensor", sem, val)

    def bank_w(b):
        B.wait_pair("tensor", B.bank_free[b])

    def consume(eng, b):
        B.wait_pair(eng, B.bank_done[b])

    X = mybir.AxisListType.X

    for sb_, dr in [(id128b, id128bD), (id128f, id128fD), (id16f, id16fD),
                    (id8f, id8fD), (trimask, trimaskD), (onesrow, onesrowD),
                    (onescol, onescolD), (lsel, lselD), (selmask, selmaskD), (sel8, sel8D), (sconst, sconstD),
                    (convw, convwD), (convb, convbD), (aparam, aparamD),
                    (dtbias, dtbiasD), (rbias, rbiasD), (windt, windtD),
                    (router, routerD)]:
        wconst = dma("sync", "SW", sb_[:], dr)
    tw("SW", wconst)
    B.wait("vector", "SW", wconst)
    B.wait("scalar", "SW", wconst)
    vec("memset", ap=ones16[:], constant=1.0)
    vec("memset", inc=True, ap=carry[:], constant=0.0)

    bank_w(0)
    for k in range(8):
        v = mm(pb[0][0:1, 0:16], onescol[:], windt[:, k * 16:(k + 1) * 16],
               start=(k == 0), stop=(k == 7), inc=(k == 7))
    B.bank_done[0] = ("ST", v)
    consume("scalar", 0)
    vs = sca("activation", inc=True, out=w1dt[:], in_=pb[0][0:1, 0:16], func=AF.Copy)
    B.bank_free[0] = ("SS", vs)

    slot_free = {i: None for i in range(4)}
    for blk in range(4):
        c0 = blk * ABLK
        for k in range(8):
            sl = k % 2
            B.wait_pair("gpsimd", slot_free[sl])
            vx = dma("gpsimd", f"SX{sl}", xk[:, sl * ABLK:(sl + 1) * ABLK],
                     xotD[k * 128:(k + 1) * 128, c0:c0 + ABLK])
            B.wait("scalar", f"SX{sl}", vx)
            vq = sca("activation", inc=True, out=sq[:, sl * ABLK:(sl + 1) * ABLK],
                     in_=xk[:, sl * ABLK:(sl + 1) * ABLK], func=AF.Square)
            tw(f"SX{sl}", vx)
            if k == 0:
                bank_w(0); bank_w(1)
            mm(pb[0][0:1, 0:ABLK], onescol[:], xk[:, sl * ABLK:(sl + 1) * ABLK],
               start=(k == 0), stop=(k == 7))
            B.wait("tensor", "SS", vq)
            v1 = mm(pb[1][0:1, 0:ABLK], onescol[:], sq[:, sl * ABLK:(sl + 1) * ABLK],
                    start=(k == 0), stop=(k == 7), inc=True)
            slot_free[sl] = ("ST", v1)
        B.bank_done[0] = ("ST", v1); B.bank_done[1] = ("ST", v1)
        consume("scalar", 0); consume("scalar", 1)
        sca("activation", out=mrow[:], in_=pb[0][0:1, 0:ABLK], func=AF.Copy, scale=sconst[0:1, 1:2])
        vs = sca("activation", inc=True, out=msqrow[:], in_=pb[1][0:1, 0:ABLK],
                 func=AF.Copy, scale=sconst[0:1, 1:2])
        B.bank_free[0] = ("SS", vs); B.bank_free[1] = ("SS", vs)
        B.wait("scalar", "SS", vs)
        vs2 = sca("activation", inc=True, out=m2row[:], in_=mrow[:], func=AF.Square)
        B.wait("vector", "SS", vs2)
        vv = vec("tensor_sub", inc=True, out=varrow[:], in0=msqrow[:], in1=m2row[:])
        B.wait("scalar", "SV", vv)
        vln1 = sca("activation", inc=True, out=sdrow[:], in_=varrow[:], func=AF.Ln,
                   bias=sconst[0:1, 0:1])
        B.wait("scalar", "SS", vln1)
        vs3 = sca("activation", inc=True, out=rrow[:], in_=sdrow[:], func=AF.Exp,
                  scale=sconst[0:1, 5:6])
        tw("SS", vs3)
        bank_w(2); bank_w(3)
        v2 = mm(pb[2][:, 0:ABLK], onesrow[:], mrow[:], True, True, inc=True)
        v3 = mm(pb[3][:, 0:ABLK], onesrow[:], rrow[:], True, True, inc=True)
        B.bank_done[2] = ("ST", v2); B.bank_done[3] = ("ST", v3)
        consume("scalar", 2); consume("scalar", 3)
        sca("activation", out=mbc[:], in_=pb[2][:, 0:ABLK], func=AF.Copy)
        vsb = sca("activation", inc=True, out=rbc[:], in_=pb[3][:, 0:ABLK], func=AF.Copy)
        B.bank_free[2] = ("SS", vsb); B.bank_free[3] = ("SS", vsb)
        vxnr = None
        for k in range(8):
            sl = k % 2
            B.wait_pair("gpsimd", slot_free[sl])
            vx = dma("gpsimd", f"SX{sl}", xk[:, sl * ABLK:(sl + 1) * ABLK],
                     xotD[k * 128:(k + 1) * 128, c0:c0 + ABLK])
            B.wait("vector", f"SX{sl}", vx)
            B.wait("vector", "SS", vsb)
            col = k * TOWN + c0
            vec("tensor_sub", out=tmpA[:], in0=xk[:, sl * ABLK:(sl + 1) * ABLK], in1=mbc[:])
            vxnr = vec("tensor_mul", inc=True, out=xnr[:, k * ABLK:(k + 1) * ABLK],
                       in0=tmpA[:], in1=rbc[:])
            vec("tensor_copy", out=xnb[:, col:col + ABLK],
                in_=xnr[:, k * ABLK:(k + 1) * ABLK])
            vv = vec("tensor_copy", inc=True, out=acc[:, col:col + ABLK],
                     in_=xk[:, sl * ABLK:(sl + 1) * ABLK])
            slot_free[sl] = ("SV", vv)
        blk = c0 // ABLK
        tw("SV", B.cnt["SV"])
        bank_w(0)
        for k in range(8):
            v = mm(pb[0][0:8, 0:ABLK], router[:, k * 8:(k + 1) * 8],
                   xnr[:, k * ABLK:(k + 1) * ABLK],
                   start=(k == 0), stop=(k == 7), inc=(k == 7))
        B.bank_done[0] = ("ST", v)
        consume("scalar", 0)
        vsig = sca("activation", inc=True, out=probs_fm[:],
                   in_=pb[0][0:8, 0:ABLK], func=AF.Sigmoid, bias=rbias[:])
        B.bank_free[0] = ("SS", vsig)
        for tts in range(4):
            tt = blk * 4 + tts
            cc0 = tts * 128
            b6, b7 = (2, 3) if tt % 2 == 0 else (3, 2)
            B.wait("tensor", "SS", vsig)
            bank_w(b6)
            v = tp(pb[b6][:, 0:8], probs_fm[:, cc0:cc0 + 128], id8f[:], inc=True)
            B.bank_done[b6] = ("ST", v)
            consume("vector", b6)
            vv = vec("tensor_copy", inc=True, out=ptm[:], in_=pb[b6][:, 0:8])
            B.bank_free[b6] = ("SV", vv)
            vv = vec("reduce_max", inc=True, out=p1col[:], in_=ptm[:], axis=X)
            B.wait("vector", "SV", vv)
            vec("tensor_single_scalar", out=eqm[:], in_=ptm[:], scalar=p1col[:], op=OP.is_equal)
            vec("scalar_tensor_tensor", out=eqm[:], in0=eqm[:], scalar=-1e30, in1=ptm[:],
                op0=OP.mult, op1=OP.add)
            vv = vec("reduce_max", inc=True, out=p2col[:], in_=eqm[:], axis=X)
            vec("tensor_add", out=s12[:], in0=p1col[:], in1=p2col[:])
            vv2 = vec("reciprocal", inc=True, out=rs12[:], in_=s12[:])
            B.wait("vector", "SV", vv2)
            vec("tensor_single_scalar", out=wtm[:], in_=ptm[:], scalar=p2col[:], op=OP.is_ge)
            vec("tensor_mul", out=wtm[:], in0=wtm[:], in1=ptm[:])
            vw = vec("tensor_scalar_mul", inc=True, out=wtm[:], in0=wtm[:], scalar1=rs12[:])
            tw("SV", vw)
            bank_w(b7)
            v = tp(pb[b7][0:8, 0:128], wtm[:], id128f[:], inc=True)
            B.bank_done[b7] = ("ST", v)
            consume("vector", b7)
            vv = vec("tensor_copy", inc=True, out=wT[:, tt * 128:tt * 128 + 128],
                     in_=pb[b7][0:8, 0:128])
            B.bank_free[b7] = ("SV", vv)
    wt_ready = B.cnt["SV"]

    ring = {"g": {0: None, 1: None}, "u": {0: None, 1: None}, "d": {0: None, 1: None},
            "gb": {0: None, 1: None}, "ub": {0: None, 1: None}, "db": {0: None, 1: None}}
    for e in range(9):
        shared = (e == 8)
        srcg = sgD if shared else wgD[e]
        srcu = suD if shared else wuD[e]
        srcda = sdaD if shared else wdaD[e]
        srcdb = sdbD if shared else wdbD[e]
        for blk in range(4):
            c0 = blk * ABLK
            wbc_ready = None
            if not shared:
                tw("SV", wt_ready)
                bank_w(1)
                v = mm(pb[1][:, 0:ABLK], sel8[:, e * 128:(e + 1) * 128],
                       wT[:, c0:c0 + ABLK], True, True, inc=True)
                B.bank_done[1] = ("ST", v)
                consume("vector", 1)
                wbc_ready = vec("tensor_copy", inc=True, out=wbc[:], in_=pb[1][:, 0:ABLK])
                B.bank_free[1] = ("SV", wbc_ready)
            hmarks = [None] * 16
            ringb = ring["gb"]
            ringub = ring["ub"]
            for m in range(16):
                sl = m % 2
                B.wait_pair("sync", ring["g"][sl])
                vwg = dma("sync", f"SG{sl}", rgw[:, sl * 1024:(sl + 1) * 1024], srcg[m])
                B.wait_pair("sync", ring["u"][sl])
                vwu = dma("sync", f"SU{sl}", rgu[:, sl * 1024:(sl + 1) * 1024], srcu[m])
                bg, bu = 4 + sl, 2 + sl
                B.wait("scalar", f"SG{sl}", vwg)
                B.wait_pair("scalar", ringb[sl])
                vcg = sca("activation", inc=True, notrack=True,
                          out=rgwb[:, sl * 1024:(sl + 1) * 1024],
                          in_=rgw[:, sl * 1024:(sl + 1) * 1024], func=AF.Copy)
                ring["g"][sl] = ("SS", vcg)
                B.wait("vector", f"SU{sl}", vwu)
                B.wait_pair("vector", ringub[sl])
                vcu = vec("tensor_copy", inc=True, notrack=True,
                          out=rgub[:, sl * 1024:(sl + 1) * 1024],
                          in_=rgu[:, sl * 1024:(sl + 1) * 1024])
                ring["u"][sl] = ("SV", vcu)
                B.wait("tensor", "SS", vcg)
                bank_w(bg)
                for k in range(8):
                    vG = mm(pb[bg][:, 0:ABLK],
                            rgwb[:, sl * 1024 + k * 128: sl * 1024 + (k + 1) * 128],
                            xnb[:, k * TOWN + c0: k * TOWN + c0 + ABLK],
                            start=(k == 0), stop=(k == 7), inc=(k == 7))
                ringb[sl] = ("ST", vG)
                B.bank_done[bg] = ("ST", vG)
                B.wait("tensor", "SV", vcu)
                bank_w(bu)
                for k in range(8):
                    vU = mm(pb[bu][:, 0:ABLK],
                            rgub[:, sl * 1024 + k * 128: sl * 1024 + (k + 1) * 128],
                            xnb[:, k * TOWN + c0: k * TOWN + c0 + ABLK],
                            start=(k == 0), stop=(k == 7), inc=(k == 7))
                ringub[sl] = ("ST", vU)
                B.bank_done[bu] = ("ST", vU)
                consume("scalar", bg)
                vsil = sca("activation", inc=True, out=silu_sb[:, sl * ABLK:(sl + 1) * ABLK],
                           in_=pb[bg][:, 0:ABLK], func=AF.Silu)
                B.bank_free[bg] = ("SS", vsil)
                B.wait("vector", "SS", vsil)
                consume("vector", bu)
                hc = m * ABLK
                if shared:
                    vh = vec("tensor_mul", inc=True, out=hb[:, hc:hc + ABLK],
                             in0=silu_sb[:, sl * ABLK:(sl + 1) * ABLK],
                             in1=pb[bu][:, 0:ABLK])
                else:
                    B.wait("vector", "SV", wbc_ready)
                    vec("tensor_mul", out=hb[:, hc:hc + ABLK],
                        in0=silu_sb[:, sl * ABLK:(sl + 1) * ABLK], in1=pb[bu][:, 0:ABLK])
                    vh = vec("tensor_mul", inc=True, out=hb[:, hc:hc + ABLK],
                             in0=hb[:, hc:hc + ABLK], in1=wbc[:])
                B.bank_free[bu] = ("SV", vh)
                hmarks[m] = vh
            vstop = {}
            for i in range(16):
                sl = i % 2
                B.wait_pair("sync", ring["d"][sl])
                vwd = dma("sync", f"SD{sl}", rgd[:, sl * 1024:sl * 1024 + 768], srcda[i])
                B.wait("scalar", f"SD{sl}", vwd)
                B.wait_pair("scalar", ring["db"][sl])
                vcd = sca("activation", inc=True, notrack=True,
                          out=rgdb[:, sl * 1024:sl * 1024 + 768],
                          in_=rgd[:, sl * 1024:sl * 1024 + 768], func=AF.Copy)
                ring["d"][sl] = ("SS", vcd)
                B.wait("tensor", "SS", vcd)
                tw("SV", hmarks[i])
                if i == 0:
                    for bbb in range(6):
                        bank_w(bbb)
                for dmj in range(6):
                    v = mm(pb[dmj][:, 0:ABLK],
                           rgdb[:, sl * 1024 + dmj * 128: sl * 1024 + (dmj + 1) * 128],
                           hb[:, i * ABLK:(i + 1) * ABLK],
                           start=(i == 0), stop=(i == 15), inc=(dmj == 5 or i == 15))
                    if i == 15:
                        vstop[dmj] = v
                ring["db"][sl] = ("ST", v)
            for dmj in range(6):
                B.bank_done[dmj] = ("ST", vstop[dmj])
                consume("vector", dmj)
                col = dmj * TOWN + c0
                vv = vec("tensor_add", inc=True, out=acc[:, col:col + ABLK],
                         in0=pb[dmj][:, 0:ABLK], in1=acc[:, col:col + ABLK])
                B.bank_free[dmj] = ("SV", vv)
            for i in range(16):
                sl = i % 2
                B.wait_pair("sync", ring["d"][sl])
                vwd = dma("sync", f"SD{sl}", rgd[:, sl * 1024 + 768:sl * 1024 + 1024], srcdb[i])
                B.wait("scalar", f"SD{sl}", vwd)
                B.wait_pair("scalar", ring["db"][sl])
                vcd = sca("activation", inc=True, notrack=True,
                          out=rgdb[:, sl * 1024 + 768:sl * 1024 + 1024],
                          in_=rgd[:, sl * 1024 + 768:sl * 1024 + 1024], func=AF.Copy)
                ring["d"][sl] = ("SS", vcd)
                B.wait("tensor", "SS", vcd)
                if i == 0:
                    bank_w(0); bank_w(1)
                for dmj in range(2):
                    v = mm(pb[dmj][:, 0:ABLK],
                           rgdb[:, sl * 1024 + 768 + dmj * 128: sl * 1024 + 768 + (dmj + 1) * 128],
                           hb[:, i * ABLK:(i + 1) * ABLK],
                           start=(i == 0), stop=(i == 15), inc=(dmj == 1 or i == 15))
                    if i == 15:
                        vstop[dmj] = v
                ring["db"][sl] = ("ST", v)
            for dmj in range(2):
                B.bank_done[dmj] = ("ST", vstop[dmj])
                consume("vector", dmj)
                col = (6 + dmj) * TOWN + c0
                vv = vec("tensor_add", inc=True, out=acc[:, col:col + ABLK],
                         in0=pb[dmj][:, 0:ABLK], in1=acc[:, col:col + ABLK])
                B.bank_free[dmj] = ("SV", vv)

    B.wait("tensor", "ST", B.cnt["ST"])
    pa_t = B.inc("ST")
    B.emit("tensor", lambda E, sh=sems["ST"]: E.nop().then_inc(sh, 1))
    B.wait("vector", "ST", pa_t)
    vec("memset", ap=xbcn[:], constant=0.0)
    vec("memset", ap=r_sb[:], constant=0.0)
    vec("memset", ap=rb16[:], constant=0.0)
    vbar = vec("memset", inc=True, ap=carry[:], constant=0.0)
    B.wait("scalar", "SV", vbar)
    B.wait("tensor", "SV", vbar)
    B.wait("sync", "SV", vbar)
    B.wait("gpsimd", "SV", vbar)

    slot_free = {i: None for i in range(4)}
    woutring = {0: None, 1: None}
    woutringb = {0: None, 1: None}
    outst = {mo: None for mo in range(8)}
    conv_mark_prev = None
    va0 = vlnn = vconv = None

    for blk in range(NBB):
        c0 = blk * BBLK
        for k in range(8):
            s0, sb_ = k % 2, 2 + k % 2
            B.wait_pair("gpsimd", slot_free[s0])
            vx0 = dma("gpsimd", f"SX{s0}", xk[:, s0 * BBLK:(s0 + 1) * BBLK],
                      x0tD[k * 128:(k + 1) * 128, c0:c0 + BBLK])
            B.wait_pair("gpsimd", slot_free[sb_])
            vxb = dma("gpsimd", f"SX{sb_}", xk[:, sb_ * BBLK:(sb_ + 1) * BBLK],
                      xbtD[k * 128:(k + 1) * 128, c0:c0 + BBLK])
            B.wait("scalar", f"SX{s0}", vx0)
            vq0 = sca("activation", inc=True, out=sq[:, s0 * BBLK:(s0 + 1) * BBLK],
                      in_=xk[:, s0 * BBLK:(s0 + 1) * BBLK], func=AF.Square)
            B.wait("scalar", f"SX{sb_}", vxb)
            vqb = sca("activation", inc=True, out=sq[:, sb_ * BBLK:(sb_ + 1) * BBLK],
                      in_=xk[:, sb_ * BBLK:(sb_ + 1) * BBLK], func=AF.Square)
            tw(f"SX{s0}", vx0)
            tw(f"SX{sb_}", vxb)
            if k == 0:
                for bbb in (0, 1, 2, 3, 5):
                    bank_w(bbb)
            mm(pb[0][0:1, 0:BBLK], onescol[:], xk[:, s0 * BBLK:(s0 + 1) * BBLK],
               start=(k == 0), stop=(k == 7))
            B.wait("tensor", "SS", vq0)
            v1 = mm(pb[1][0:1, 0:BBLK], onescol[:], sq[:, s0 * BBLK:(s0 + 1) * BBLK],
                    start=(k == 0), stop=(k == 7), inc=True)
            slot_free[s0] = ("ST", v1)
            mm(pb[2][0:1, 0:BBLK], onescol[:], xk[:, sb_ * BBLK:(sb_ + 1) * BBLK],
               start=(k == 0), stop=(k == 7))
            B.wait("tensor", "SS", vqb)
            mm(pb[3][0:1, 0:BBLK], onescol[:], sq[:, sb_ * BBLK:(sb_ + 1) * BBLK],
               start=(k == 0), stop=(k == 7))
            vd = mm(pb[5][0:16, 0:BBLK], windt[:, k * 16:(k + 1) * 16],
                    xk[:, sb_ * BBLK:(sb_ + 1) * BBLK],
                    start=(k == 0), stop=False, inc=True)
            slot_free[sb_] = ("ST", vd)
        vstats = B.cnt["ST"]
        for bq in range(4):
            B.bank_done[bq] = ("ST", vstats)
            consume("scalar", bq)
        sca("activation", out=mrow[:, 0:BBLK], in_=pb[0][0:1, 0:BBLK], func=AF.Copy,
            scale=sconst[0:1, 1:2])
        sca("activation", out=msqrow[:, 0:BBLK], in_=pb[1][0:1, 0:BBLK], func=AF.Copy,
            scale=sconst[0:1, 1:2])
        sca("activation", out=negmb[:], in_=pb[2][0:1, 0:BBLK], func=AF.Copy,
            scale=sconst[0:1, 2:3])
        vs0 = sca("activation", inc=True, out=msqb[:], in_=pb[3][0:1, 0:BBLK],
                  func=AF.Copy, scale=sconst[0:1, 1:2])
        for bq in range(4):
            B.bank_free[bq] = ("SS", vs0)
        B.wait("scalar", "SS", vs0)
        sca("activation", out=m2row[:, 0:BBLK], in_=mrow[:, 0:BBLK], func=AF.Square)
        vs1 = sca("activation", inc=True, out=mb2row[:], in_=negmb[:], func=AF.Square)
        B.wait("vector", "SS", vs1)
        vec("tensor_sub", out=varrow[:, 0:BBLK], in0=msqrow[:, 0:BBLK], in1=m2row[:, 0:BBLK])
        vv = vec("tensor_sub", inc=True, out=varb[:], in0=msqb[:], in1=mb2row[:])
        B.wait("scalar", "SV", vv)
        sca("activation", out=sdrow[:, 0:BBLK], in_=varrow[:, 0:BBLK], func=AF.Ln,
            bias=sconst[0:1, 0:1])
        vln2 = sca("activation", inc=True, out=sdb[:], in_=varb[:], func=AF.Ln,
                   bias=sconst[0:1, 0:1])
        B.wait("scalar", "SS", vln2)
        sca("activation", out=rrow[:, 0:BBLK], in_=sdrow[:, 0:BBLK], func=AF.Exp,
            scale=sconst[0:1, 5:6])
        vrec = sca("activation", inc=True, out=rbrow[:], in_=sdb[:], func=AF.Exp,
                   scale=sconst[0:1, 5:6])
        B.wait("tensor", "SS", vs0)
        vd = mm(pb[5][0:16, 0:BBLK], w1dt[:], negmb[:], start=False, stop=True, inc=True)
        B.bank_done[5] = ("ST", vd)
        tw("SS", vrec)
        bank_w(2); bank_w(3); bank_w(4)
        v2 = mm(pb[2][:, 0:BBLK], onesrow[:], mrow[:, 0:BBLK], True, True, inc=True)
        v3 = mm(pb[3][:, 0:BBLK], onesrow[:], rrow[:, 0:BBLK], True, True, inc=True)
        v4 = mm(pb[4][:, 0:BBLK], onesrow[:], rbrow[:], True, True, inc=True)
        B.bank_done[2] = ("ST", v2); B.bank_done[3] = ("ST", v3); B.bank_done[4] = ("ST", v4)
        consume("scalar", 2); consume("scalar", 3); consume("scalar", 4)
        sca("activation", out=mbc[:, 0:BBLK], in_=pb[2][:, 0:BBLK], func=AF.Copy)
        sca("activation", out=rbc[:, 0:BBLK], in_=pb[3][:, 0:BBLK], func=AF.Copy)
        vs3 = sca("activation", inc=True, out=rbbc[:], in_=pb[4][:, 0:BBLK], func=AF.Copy)
        B.bank_free[2] = ("SS", vs3); B.bank_free[3] = ("SS", vs3); B.bank_free[4] = ("SS", vs3)
        consume("vector", 5)
        B.wait("vector", "SS", vs3)
        vv = vec("tensor_mul", inc=True, out=dtrb_sb[:], in0=pb[5][0:16, 0:BBLK],
                 in1=rbbc[0:16, :])
        B.bank_free[5] = ("SV", vv)
        B.wait("scalar", "SV", vv)
        vsp1 = sca("activation", inc=True, out=eab_sb[:], in_=dtrb_sb[:], func=AF.Exp,
                   bias=dtbias[:])
        B.wait("scalar", "SS", vsp1)
        vsp2 = sca("activation", inc=True, out=dtb_sb[:], in_=eab_sb[:], func=AF.Ln,
                   bias=sconst[0:16, 4:5])
        B.wait("scalar", "SS", vsp2)
        vse = sca("activation", inc=True, out=eab_sb[:], in_=dtb_sb[:], func=AF.Exp,
                  scale=aparam[:])
        B.wait("vector", "SS", vse)
        vsc = vscan(norm_sb[:, 0:128], eab_sb[:, 0:128], ones16[:], carry[:])
        B.wait("vector", "SV", vsc)
        vsc2 = vscan(norm_sb[:, 128:256], eab_sb[:, 128:256], ones16[:],
                     norm_sb[:, 127:128])
        B.wait("vector", "SV", vsc2)
        vcar = vec("tensor_copy", inc=True, out=carry[:], in_=norm_sb[:, 255:256])
        B.wait("scalar", "SV", vcar)
        vlnn = sca("activation", inc=True, out=lnn_sb[:], in_=norm_sb[:], func=AF.Ln)
        for k in range(8):
            s0 = k % 2
            B.wait_pair("gpsimd", slot_free[s0])
            vx0 = dma("gpsimd", f"SX{s0}", xk[:, s0 * BBLK:(s0 + 1) * BBLK],
                      x0tD[k * 128:(k + 1) * 128, c0:c0 + BBLK])
            B.wait("vector", f"SX{s0}", vx0)
            B.wait("vector", "SS", vs3)
            vec("tensor_sub", out=tmpA[:, 0:BBLK], in0=xk[:, s0 * BBLK:(s0 + 1) * BBLK],
                in1=mbc[:, 0:BBLK])
            vec("tensor_mul", out=xn0[:, k * BBLK:(k + 1) * BBLK],
                in0=tmpA[:, 0:BBLK], in1=rbc[:, 0:BBLK])
            vv = vec("tensor_copy", inc=True, out=xn0b[:, k * BBLK:(k + 1) * BBLK],
                     in_=xn0[:, k * BBLK:(k + 1) * BBLK])
            slot_free[s0] = ("SV", vv)
        vxn0 = B.cnt["SV"]
        tw("SV", vxn0)
        bank_w(5)
        for k in range(8):
            vd0 = mm(pb[5][0:16, 0:BBLK], windt[:, k * 16:(k + 1) * 16],
                     xn0[:, k * BBLK:(k + 1) * BBLK],
                     start=(k == 0), stop=(k == 7), inc=(k == 7))
        B.bank_done[5] = ("ST", vd0)
        consume("scalar", 5)
        vs5 = sca("activation", inc=True, out=dtr0_sb[:], in_=pb[5][0:16, 0:BBLK],
                  func=AF.Copy)
        B.bank_free[5] = ("SS", vs5)
        B.wait("scalar", "SS", vs5)
        vs6 = sca("activation", inc=True, out=norm_sb[:], in_=dtr0_sb[:], func=AF.Exp,
                  bias=dtbias[:])
        B.wait("scalar", "SS", vs6)
        vsp = sca("activation", inc=True, out=dt0_sb[:], in_=norm_sb[:],
                  func=AF.Ln, bias=sconst[0:16, 4:5])
        B.wait("vector", "SS", vsp)
        va0 = vec("tensor_scalar_mul", inc=True, out=a0_sb[:], in0=dt0_sb[:],
                  scalar1=aparam[:])
        prev_conv = conv_mark_prev
        for c in range(2):
            ch0 = c * CH
            B.wait("vector", "SV", va0)
            vsc = vscan(cuma[:], ones16[:], a0_sb[:, ch0:ch0 + CH], 0.0)
            B.wait("vector", "SS", vlnn)
            vpsi = vec("tensor_sub", inc=True, out=psiin[:], in0=cuma[:],
                       in1=lnn_sb[:, ch0:ch0 + CH])
            sca("activation", out=fexp[:], in_=cuma[:], func=AF.Exp, scale=sconst[0:16, 3:4])
            sca("activation", out=dexp[:], in_=cuma[:], func=AF.Exp, scale=sconst[0:16, 3:4],
                bias=cuma[:, CH - 1:CH])
            sca("activation", out=lam[:], in_=cuma[:, CH - 1:CH], func=AF.Exp)
            vsx = sca("activation", inc=True, out=psie[:], in_=psiin[:], func=AF.Exp)
            vls = vec("tensor_scalar_mul", inc=True, out=lamsel[:], in0=selmask[:],
                      scalar1=lam[:])
            tw("SS", vsx); tw("SV", vls)
            bank_w(4)
            tp(pb[4][:, 0:16], fexp[:], id16f[:])
            tp(pb[4][:, 16:32], dexp[:], id16f[:])
            tp(pb[4][:, 32:48], psie[:], id16f[:])
            vt4 = mm(pb[4][:, 48:56], lsel[:], lamsel[:], True, True, inc=True)
            B.bank_done[4] = ("ST", vt4)
            consume("vector", 4)
            vec("tensor_copy", out=ft_sb[:, c * 16:(c + 1) * 16], in_=pb[4][:, 0:16])
            vec("tensor_copy", out=dtt_sb[:, c * 16:(c + 1) * 16], in_=pb[4][:, 16:32])
            vec("tensor_copy", out=psit_sb[:, c * 16:(c + 1) * 16], in_=pb[4][:, 32:48])
            vsm = vec("tensor_copy", inc=True, out=lamx_sb[:, c * 8:(c + 1) * 8],
                      in_=pb[4][:, 48:56])
            B.bank_free[4] = ("SV", vsm)
        ftorder = list(range(8, 24)) + list(range(8))
        proj_mark = {}
        conv_mark = {}
        for mt in ftorder:
            sl = mt % 2
            B.wait_pair("sync", ring["g"][sl])
            vw = dma("sync", f"SG{sl}", rgw[:, sl * 1024:(sl + 1) * 1024], wingD[mt])
            bk = sl
            B.wait("scalar", f"SG{sl}", vw)
            B.wait_pair("scalar", ring["gb"][sl])
            vcg = sca("activation", inc=True, notrack=True,
                      out=rgwb[:, sl * 1024:(sl + 1) * 1024],
                      in_=rgw[:, sl * 1024:(sl + 1) * 1024], func=AF.Copy)
            ring["g"][sl] = ("SS", vcg)
            B.wait("tensor", "SS", vcg)
            bank_w(bk)
            for k in range(8):
                vp = mm(pb[bk][:, 0:BBLK],
                        rgwb[:, sl * 1024 + k * 128: sl * 1024 + (k + 1) * 128],
                        xn0b[:, k * BBLK:(k + 1) * BBLK],
                        start=(k == 0), stop=(k == 7), inc=(k == 7))
            ring["gb"][sl] = ("ST", vp)
            B.bank_done[bk] = ("ST", vp)
            consume("scalar", bk)
            col = mt * (BBLK + 3) + 3
            if prev_conv is not None:
                B.wait_pair("scalar", prev_conv[mt])
            vpe = sca("activation", inc=True, notrack=True,
                      out=xbcn[:, col:col + BBLK],
                      in_=pb[bk][:, 0:BBLK], func=AF.Copy)
            B.bank_free[bk] = ("SS", vpe)
            proj_mark[mt] = ("SS", vpe)
        tapm = {}
        for grp in range(0, 24, 8):
            fts = ftorder[grp:grp + 8]
            for kk in range(4):
                for ft in fts:
                    cin = ft * (BBLK + 3)
                    cout = ft * BBLK
                    if kk == 0:
                        B.wait_pair("vector", proj_mark[ft])
                        v = vec("tensor_scalar", notrack=True,
                                out=convo[:, cout:cout + BBLK],
                                in0=xbcn[:, cin:cin + BBLK],
                                scalar1=convw[:, ft * 4:ft * 4 + 1],
                                scalar2=convb[:, ft:ft + 1], op0=OP.mult, op1=OP.add)
                    else:
                        B.wait("vector", "SV", tapm[ft])
                        v = vec("scalar_tensor_tensor", notrack=True,
                                out=convo[:, cout:cout + BBLK],
                                in0=xbcn[:, cin + kk:cin + kk + BBLK],
                                scalar=convw[:, ft * 4 + kk:ft * 4 + kk + 1],
                                in1=convo[:, cout:cout + BBLK], op0=OP.mult, op1=OP.add)
                    tapm[ft] = v
                    if kk == 3:
                        conv_mark[ft] = ("SV", v)
        vconv = vec("tensor_copy", inc=True, notrack=True,
                    out=xbcn.ap().rearrange("p (f c) -> p f c", f=24)[:, :, 0:3],
                    in_=xbcn.ap().rearrange("p (f c) -> p f c", f=24)[:, :, BBLK:BBLK + 3])
        conv_mark_prev = dict(conv_mark)
        for c in range(2):
            ch0 = c * CH
            for i in range(8):
                bt = 8 + i % 2
                B.wait_pair("tensor", conv_mark[16 + i])
                bank_w(bt)
                cc = (16 + i) * BBLK + ch0
                v = tp(pbB[i % 2][:, 0:128], convo[:, cc:cc + CH], id128b[:], inc=True)
                B.bank_done[bt] = ("ST", v)
                consume("vector", bt)
                vv = vec("tensor_copy", inc=True, out=xt_sb[:, i * 128:(i + 1) * 128],
                         in_=pbB[i % 2][:, 0:128])
                B.bank_free[bt] = ("SV", vv)
            for i in range(8):
                bt = 8 + i % 2
                B.wait_pair("tensor", conv_mark[8 + i])
                bank_w(bt)
                cc = (8 + i) * BBLK + ch0
                v = tp(pbB[i % 2][:, 0:128], convo[:, cc:cc + CH], id128b[:], inc=True)
                B.bank_done[bt] = ("ST", v)
                consume("vector", bt)
                vec("tensor_scalar_mul", out=bdt_sb[:, i * 128:i * 128 + 64],
                    in0=pbB[i % 2][:, 0:64], scalar1=dtt_sb[:, c * 16 + 2 * i:c * 16 + 2 * i + 1])
                vv = vec("tensor_scalar_mul", inc=True,
                         out=bdt_sb[:, i * 128 + 64:(i + 1) * 128],
                         in0=pbB[i % 2][:, 64:128], scalar1=dtt_sb[:, c * 16 + 2 * i + 1:c * 16 + 2 * i + 2])
                B.bank_free[bt] = ("SV", vv)
            vxb_ready = B.cnt["SV"]
            tw("SV", vxb_ready)
            vgm_all = None
            for h in range(16):
                j, hh = h // 2, h % 2
                pbase = hh * 64
                bG = 4 + h % 2
                ftB = (8 + j) * BBLK + ch0
                ftC = j * BBLK + ch0
                B.wait_pair("tensor", conv_mark[j])
                bank_w(bG)
                vg = mm(pb[bG][:, 0:128],
                        convo[pbase:pbase + 64, ftB:ftB + CH],
                        convo[pbase:pbase + 64, ftC:ftC + CH],
                        True, True, inc=True)
                B.bank_done[bG] = ("ST", vg)
                consume("vector", bG)
                vgm_all = vec("scalar_tensor_tensor", inc=True,
                              out=gmt_sb[:, h * 128:(h + 1) * 128],
                              in0=pb[bG][:, 0:128], scalar=ft_sb[:, c * 16 + h:c * 16 + h + 1],
                              in1=trimask[:], op0=OP.mult, op1=OP.mult)
                B.bank_free[bG] = ("SV", vgm_all)
            tw("SV", vgm_all)
            for j in range(8):
                bY = 2 + j % 2
                bS = 0 + j % 2
                vy = vs_ = None
                for hh in range(2):
                    h = 2 * j + hh
                    pbase = hh * 64
                    ftC = j * BBLK + ch0
                    if hh == 0:
                        bank_w(bY)
                        bank_w(bS)
                    mm(pb[bY][:, pbase:pbase + 64], gmt_sb[:, h * 128:(h + 1) * 128],
                       xt_sb[:, j * 128 + pbase: j * 128 + pbase + 64],
                       start=True, stop=False)
                    vy = mm(pb[bY][:, pbase:pbase + 64],
                            convo[pbase:pbase + 64, ftC:ftC + CH],
                            rb16[pbase:pbase + 64, j * 64:(j + 1) * 64],
                            start=False, stop=True, inc=True)
                    vs_ = mm(pb[bS][0:64, pbase:pbase + 64],
                             bdt_sb[:, j * 128 + pbase: j * 128 + pbase + 64],
                             xt_sb[:, j * 128 + pbase: j * 128 + pbase + 64],
                             True, True, inc=True)
                B.bank_done[bY] = ("ST", vy)
                B.bank_done[bS] = ("ST", vs_)
                consume("vector", bY)
                for hh in range(2):
                    h = 2 * j + hh
                    vv = vec("tensor_scalar_mul", inc=(hh == 1),
                             out=ztm[:, j * 128 + hh * 64: j * 128 + (hh + 1) * 64],
                             in0=pb[bY][:, hh * 64:(hh + 1) * 64],
                             scalar1=psit_sb[:, c * 16 + h:c * 16 + h + 1])
                B.bank_free[bY] = ("SV", vv)
                consume("vector", bS)
                for hh in range(2):
                    vv = vec("scalar_tensor_tensor", inc=(hh == 1),
                             out=r_sb[hh * 64:(hh + 1) * 64, j * 64:(j + 1) * 64],
                             in0=r_sb[hh * 64:(hh + 1) * 64, j * 64:(j + 1) * 64],
                             scalar=lamx_sb[hh * 64:(hh + 1) * 64, c * 8 + j:c * 8 + j + 1],
                             in1=pb[bS][0:64, hh * 64:(hh + 1) * 64],
                             op0=OP.mult, op1=OP.add)
                B.bank_free[bS] = ("SV", vv)
            vrb = vec("tensor_copy", inc=True, out=rb16[:], in_=r_sb[:])
            tw("SV", vrb)
            for i in range(8):
                bt = 8 + i % 2
                bank_w(bt)
                v = tp(pbB[i % 2][:, 0:128], ztm[:, i * 128:(i + 1) * 128], id128b[:], inc=True)
                B.bank_done[bt] = ("ST", v)
                consume("vector", bt)
                vv = vec("tensor_copy", inc=True,
                         out=zfm[:, i * BBLK + ch0: i * BBLK + ch0 + CH],
                         in_=pbB[i % 2][:, 0:128])
                B.bank_free[bt] = ("SV", vv)
        vzfm = B.cnt["SV"]
        tw("SV", vzfm)
        for mo in range(8):
            sl = mo % 2
            B.wait_pair("sync", woutring[sl])
            vw = dma("sync", f"SU{sl}", rgu[:, sl * 1024:(sl + 1) * 1024], woutD[mo])
            bk = sl
            B.wait("scalar", f"SU{sl}", vw)
            B.wait_pair("scalar", woutringb[sl])
            vcw = sca("activation", inc=True, notrack=True,
                      out=rgub[:, sl * 1024:(sl + 1) * 1024],
                      in_=rgu[:, sl * 1024:(sl + 1) * 1024], func=AF.Copy)
            woutring[sl] = ("SS", vcw)
            B.wait("tensor", "SS", vcw)
            bank_w(bk)
            for ko in range(8):
                vwo = mm(pb[bk][:, 0:BBLK],
                         rgub[:, sl * 1024 + ko * 128: sl * 1024 + (ko + 1) * 128],
                         zfm[:, ko * BBLK:(ko + 1) * BBLK],
                         start=(ko == 0), stop=(ko == 7), inc=(ko == 7))
            woutringb[sl] = ("ST", vwo)
            B.bank_done[bk] = ("ST", vwo)
            consume("vector", bk)
            acol = mo * TOWN + (blk % 8) * BBLK
            B.wait_pair("vector", outst[mo])
            vv = vec("tensor_add", inc=True, out=outc_sb[:, mo * BBLK:(mo + 1) * BBLK],
                     in0=pb[bk][:, 0:BBLK], in1=acc[:, acol:acol + BBLK])
            B.bank_free[bk] = ("SV", vv)
            B.wait("gpsimd", "SV", vv)
            vso = dma("gpsimd", f"SO{mo}", outcD[mo * 128:(mo + 1) * 128, c0:c0 + BBLK],
                      outc_sb[:, mo * BBLK:(mo + 1) * BBLK])
            outst[mo] = (f"SO{mo}", vso)

    for mo in range(8):
        B.wait("gpsimd", f"SO{mo}", B.cnt[f"SO{mo}"])

    import contextlib
    with nc.allow_low_precision(reason="bf16 residual accumulator; within measured tolerance"), nc.Block() as block:
        def runner(lst):
            def go(E, lst=lst):
                for fn in lst:
                    fn(E)
            return go
        block.tensor(runner(B.prog["tensor"]))
        block.vector(runner(B.prog["vector"]))
        block.scalar(runner(B.prog["scalar"]))
        block.sync(runner(B.prog["sync"]))
        block.gpsimd(runner(B.prog["gpsimd"]))
    return nc


def _host_inputs(inputs):
    f32 = np.float32
    x = np.asarray(inputs["L0_attn_x"], f32)
    W_in = np.asarray(inputs["L0_attn_W_in"], f32)
    conv_w = np.asarray(inputs["L0_attn_conv_w"], f32)
    conv_b = np.asarray(inputs["L0_attn_conv_b"], f32)
    A_param = np.asarray(inputs["L0_attn_A_param"], f32)
    dt_bias = np.asarray(inputs["L0_attn_dt_bias"], f32)
    W_out = np.asarray(inputs["L0_attn_W_out"], f32)
    sg = np.asarray(inputs["L0_ffn_sg_w"], f32)
    su = np.asarray(inputs["L0_ffn_su_w"], f32)
    sd = np.asarray(inputs["L0_ffn_sd_w"], f32)
    rw = np.asarray(inputs["L0_ffn_router_w"], f32)
    rb = np.asarray(inputs["L0_ffn_routing_bias"], f32)
    wg = np.asarray(inputs["L0_ffn_wg"], f32)
    wu = np.asarray(inputs["L0_ffn_wu"], f32)
    wd = np.asarray(inputs["L0_ffn_wd"], f32)

    def up_tiles(w):
        return np.ascontiguousarray(
            w.T.reshape(8, 128, 16, 128).transpose(2, 1, 0, 3).reshape(16, 128, 1024))

    def down_tiles(w):
        return np.ascontiguousarray(w.T.reshape(16, 128, 1024))

    com = {}
    com["wing_d"] = np.ascontiguousarray(
        W_in[:3072].T.reshape(8, 128, 24, 128).transpose(2, 1, 0, 3).reshape(24, 128, 1024))
    com["windt_d"] = np.ascontiguousarray(
        W_in[3072:].T.reshape(8, 128, 16).transpose(1, 0, 2).reshape(128, 128))
    com["wout_d"] = np.ascontiguousarray(
        W_out.T.reshape(8, 128, 8, 128).transpose(2, 1, 0, 3).reshape(8, 128, 1024))
    com["router_d"] = np.ascontiguousarray(
        rw.T.reshape(8, 128, 8).transpose(1, 0, 2).reshape(128, 64))
    com["sg_d"] = up_tiles(sg)
    com["su_d"] = up_tiles(su)
    sdt = down_tiles(sd)
    com["sda_d"] = np.ascontiguousarray(sdt[:, :, 0:768])
    com["sdb_d"] = np.ascontiguousarray(sdt[:, :, 768:1024])
    com["wg_d"] = np.stack([up_tiles(wg[e]) for e in range(NE)])
    com["wu_d"] = np.stack([up_tiles(wu[e]) for e in range(NE)])
    wdt = np.stack([down_tiles(wd[e]) for e in range(NE)])
    com["wda_d"] = np.ascontiguousarray(wdt[:, :, :, 0:768])
    com["wdb_d"] = np.ascontiguousarray(wdt[:, :, :, 768:1024])
    com["convw_d"] = np.ascontiguousarray(
        conv_w.reshape(24, 128, 4).transpose(1, 0, 2).reshape(128, 96))
    com["convb_d"] = np.ascontiguousarray(conv_b.reshape(24, 128).T)
    com["aparam_d"] = np.ascontiguousarray(A_param[:, None])
    com["dtbias_d"] = np.ascontiguousarray(dt_bias[:, None])
    com["rbias_d"] = np.ascontiguousarray(rb[:, None])
    com["id128b_d"] = np.eye(128, dtype=ml_dtypes.bfloat16)
    com["id128f_d"] = np.eye(128, dtype=f32)
    com["id16f_d"] = np.eye(16, dtype=f32)
    com["id8f_d"] = np.eye(8, dtype=f32)
    com["trimask_d"] = np.triu(np.ones((128, 128), f32))
    com["onesrow_d"] = np.ones((1, 128), f32)
    com["onescol_d"] = np.ones((128, 1), f32)
    lsel = np.zeros((16, 128), f32)
    for h in range(16):
        if h % 2 == 0:
            lsel[h, :64] = 1.0
        else:
            lsel[h, 64:] = 1.0
    com["lsel_d"] = lsel
    selm = np.zeros((16, 8), f32)
    for h in range(16):
        selm[h, h // 2] = 1.0
    com["selmask_d"] = selm
    sel8 = np.zeros((8, 1024), ml_dtypes.bfloat16)
    for e in range(8):
        sel8[e, e * 128:(e + 1) * 128] = 1.0
    com["sel8_d"] = sel8
    sc = np.zeros((128, 6), f32)
    sc[:, 0] = EPS; sc[:, 1] = 1.0 / D; sc[:, 2] = -1.0 / D; sc[:, 3] = -1.0
    sc[:, 4] = 1.0; sc[:, 5] = -0.5
    com["sconst_d"] = sc

    x0t = np.ascontiguousarray(x[0].T)
    in_maps = []
    for core in range(8):
        b, hf = core // 2, core % 2
        t0 = hf * TOWN
        m = dict(com)
        m["x0t_d"] = x0t
        m["xbt_d"] = x0t if b == 0 else np.ascontiguousarray(x[b].T)
        m["xot_d"] = np.ascontiguousarray(x[b, t0:t0 + TOWN].T)
        in_maps.append(m)
    return in_maps


def kernel(**inputs):
    in_maps = _host_inputs(inputs)
    nc = build_program()
    res = run_bass_kernel_spmd(nc, in_maps, core_ids=list(range(8)))
    out = np.empty((4, S, D), np.float32)
    for core in range(8):
        b, hf = core // 2, core % 2
        t0 = hf * TOWN
        oc = np.asarray(res.results[core]["outc"])
        out[b, t0:t0 + TOWN] = oc[:, t0:t0 + TOWN].T
    return out
